# version 12
# baseline (speedup 1.0000x reference)
import numpy as np
import ml_dtypes
import concourse.bass as bass
import concourse.mybir as mybir
from concourse.bass_utils import run_bass_kernel_spmd
from concourse.library_overlay import lower_extended_insts

F32 = mybir.dt.float32
BF16 = mybir.dt.bfloat16
AF = mybir.ActivationFunctionType
ALU = mybir.AluOpType
AX = mybir.AxisListType
BFNP = ml_dtypes.bfloat16

NCORES = 8
SEQ, DM, NH, DK = 2048, 1024, 16, 64
NSTRIP, STRIP = 4, 512
NTILE = SEQ // 128
NCH = DM // 128
FI, ICN = 2816, 22
NE = 8
WG = 11

KB = 1024


class MK:
    def __init__(self, nseq=4, taps=()):
        self.nseq = nseq
        self.taps = set(taps)
        nc = bass.Bass("TRN2", target_bir_lowering=False)
        self.nc = nc
        self.xin = nc.dram_tensor("xin", [nseq, SEQ, DM], F32, kind="ExternalInput").ap()
        self.wqT = nc.dram_tensor("wqT", [DM, DM], F32, kind="ExternalInput").ap()
        self.wkT = nc.dram_tensor("wkT", [DM, DM], F32, kind="ExternalInput").ap()
        self.wvT = nc.dram_tensor("wvT", [DM, DM], F32, kind="ExternalInput").ap()
        self.woTp = nc.dram_tensor("woTp", [128, 8, DM], F32, kind="ExternalInput").ap()
        self.rTd = nc.dram_tensor("rTd", [DM, 8], F32, kind="ExternalInput").ap()
        self.gTd = nc.dram_tensor("gTd", [NE, DM, FI], F32, kind="ExternalInput").ap()
        self.uTd = nc.dram_tensor("uTd", [NE, DM, FI], F32, kind="ExternalInput").ap()
        self.dTd = nc.dram_tensor("dTd", [NE, FI, DM], F32, kind="ExternalInput").ap()
        self.identd = nc.dram_tensor("identd", [128, 128], BF16, kind="ExternalInput").ap()
        self.cmaskd = nc.dram_tensor("cmaskd", [128, 4 * 512], BF16, kind="ExternalInput").ap()
        self.outd = nc.dram_tensor("outd", [nseq, SEQ, DM], F32, kind="ExternalOutput").ap()
        self.x1scr = nc.dram_tensor("x1scr", [SEQ, DM], F32).ap()
        self.wgu = nc.dram_tensor("wgu", [NE, WG, 128, 4096], BF16).ap()
        self.wdn = nc.dram_tensor("wdn", [NE, WG, 128, 2048], BF16).ap()
        self.tapd = {}
        if "qkv" in self.taps:
            self.tapd["tq"] = nc.dram_tensor("tq", [128, NCH * SEQ], BF16, kind="ExternalOutput").ap()
            self.tapd["tk"] = nc.dram_tensor("tk", [128, NCH * SEQ], BF16, kind="ExternalOutput").ap()
            self.tapd["tv"] = nc.dram_tensor("tv", [128, NTILE * NH * 65], BF16, kind="ExternalOutput").ap()
        if "x1" in self.taps:
            self.tapd["tx1"] = nc.dram_tensor("tx1", [SEQ, DM], F32, kind="ExternalOutput").ap()
        if "mask" in self.taps:
            self.tapd["tmask"] = nc.dram_tensor("tmask", [128, NTILE * 8], BF16, kind="ExternalOutput").ap()
        if "lof" in self.taps:
            self.tapd["tlof"] = nc.dram_tensor("tlof", [128, NCH * 512], BF16, kind="ExternalOutput").ap()
        if "ofm" in self.taps:
            self.tapd["tofm"] = nc.dram_tensor("tofm", [128, 8 * 512], BF16, kind="ExternalOutput").ap()

        self.stream = {e: [] for e in "tvsqg"}
        self.hwm = {e: {} for e in "tvsqg"}
        self._wtok = {}
        self._rtok = {}
        self.out_toks = []

        self._ctx = []

        def E(cm):
            self._ctx.append(cm)
            return cm.__enter__()

        self._cleanup = nc.cleanup_on_exit()
        self._cleanup.__enter__()

        self.sem = {}
        for e in ["t", "v", "s", "q", "g", "w0", "w1", "w2", "w3",
                  "d0", "d1", "d2", "d3", "wb0", "wb1", "sg0", "sg1"]:
            self.sem[e] = E(nc.semaphore("sem_" + e))
        self.cnt = {e: 0 for e in self.sem}

        total = 212832 - 64
        self.arena_h = E(nc.sbuf_tensor("arena", [128, total], mybir.dt.uint8))
        self.base = nc.lookup_mloc(self.arena_h).addr
        self._off = 0
        self._sizes = {}

        def sb(name, cols, dtype, off=None):
            bsz = cols * (2 if dtype == BF16 else 4)
            if off is None:
                off = self._off
                self._off += (bsz + 63) // 64 * 64
            assert off + bsz <= total, (name, off, bsz, total)
            self._sizes[name] = (off, bsz)
            return nc.alloc_sbuf_tensor_at(name, [128, cols], dtype, offset=self.base + off)

        self.wo = sb("wo", 8192, BF16)
        self.ident = sb("ident", 128, BF16)
        self.cmask = sb("cmask", 2048, BF16)
        self.ones = sb("ones", 128, BF16)
        self.consts = sb("consts", 8, F32)
        self.rT = sb("rT", 64, BF16)
        self.rTf = sb("rTf", 64, F32)
        self.rTlo = sb("rTlo", 64, BF16)
        self.masktm = sb("masktm", NTILE * 8, BF16)
        self.stats = sb("stats", 32, F32)
        self.otmpf = sb("otmpf", 512, F32)
        self.recipb = sb("recipb", 512, BF16, off=self._sizes["otmpf"][0])
        self.mrow = sb("mrow", 512, BF16)
        self.otmp2 = sb("otmp2", 512, BF16, off=self._sizes["mrow"][0])
        self.tmerge = sb("tmerge", 1024, F32)
        self.sgl = sb("sgl", 1024, BF16)

        r1 = self._off
        self.wq = sb("wq", 8192, BF16)
        self.wbuf0 = sb("wbuf0", 2048, BF16, off=r1)
        self.wbuf2 = sb("wbuf2", 2048, BF16, off=r1 + 4096)
        self.ofm = sb("ofm", 4096, BF16, off=r1 + 8192)
        self.x1lofm = sb("x1lofm", NCH * 512, BF16, off=r1 + 8192)
        r2 = self._off
        self.xfm = sb("xfm", NCH * SEQ, BF16)
        r3 = self._off
        self.qfm = sb("qfm", NCH * SEQ, BF16)
        self.zfm = sb("zfm", ICN * 512, BF16, off=r3)
        self.xefm = sb("xefm", NCH * 512, BF16, off=r3 + 22528)
        r4 = self._off
        self.kfm = sb("kfm", NCH * SEQ, BF16)
        self.yacc = sb("yacc", 4 * DM, F32, off=r4)
        self.vtm = sb("vtm", NTILE * NH * 65, BF16)
        self.xres = sb("xres", 4 * DM, F32)
        r7 = self._off
        self.pts = sb("pts", NTILE * 512, BF16)
        self.wbuf1 = sb("wbuf1", 2048, BF16, off=r7)
        self.wbuf3 = sb("wbuf3", 2048, BF16, off=r7 + 4096)
        self.dnbuf = [sb(f"dnbuf{p}", 1024, BF16, off=r7 + 8192 + p * 2048) for p in range(4)]
        self.xcast = sb("xcast", 4096, BF16, off=r7)
        r6 = self._sizes["xres"][0]
        self.stg_gu = [sb(f"stg_gu{p}", 4096, BF16, off=r6 + p * 8192) for p in range(2)]
        self.stg_dn = [sb(f"stg_dn{p}", 2048, BF16, off=r7 + 8192 + p * 4096) for p in range(2)]
        assert self._off <= total, self._off

        self.pf = E(nc.psum_tensor("pf", [128, 6 * 512], F32))
        self.pb = E(nc.psum_tensor("pb", [128, 2 * 1024], BF16))
        self.wbufs = [self.wbuf0, self.wbuf1, self.wbuf2, self.wbuf3]

        self.block_cm = nc.Block()
        self.block = self.block_cm.__enter__()

        self._init_phase()
        self._a1(0)
        self._precast()
        pc_tok = dict(self.cnt)
        for eng in "tvsq":
            for sname in ("wb0", "wb1", "sg0", "sg1"):
                self._wait(eng, (sname, pc_tok[sname]))
        for b in range(nseq):
            if b > 0:
                self._a1(b)
            self._attn(b)
            self._ffn(b)
            self.barrier()
        self._finish()

        blk = self.block
        st = self.stream
        blk.gpsimd(lambda e: [f(e) for f in st["g"]])
        blk.vector(lambda e: [f(e) for f in st["v"]])
        blk.scalar(lambda e: [f(e) for f in st["s"]])
        blk.tensor(lambda e: [f(e) for f in st["t"]])
        blk.sync(lambda e: [f(e) for f in st["q"]])

        self.block_cm.__exit__(None, None, None)
        for cm in reversed(self._ctx):
            cm.__exit__(None, None, None)
        self._cleanup.__exit__(None, None, None)
        lower_extended_insts(nc)

    def _wait(self, eng, tok):
        if tok is None:
            return
        s, v = tok
        if v <= 0 or self.hwm[eng].get(s, 0) >= v:
            return
        self.hwm[eng][s] = v
        sem = self.sem[s]
        self.stream[eng].append(lambda E, sem=sem, v=v: E.wait_ge(sem, v))

    ENG_SEM = {"t": "t", "v": "v", "s": "s", "q": "q", "g": "g"}

    def op(self, eng, fn, rd=(), wr=(), extra=(), inc=True, dsem=None):
        own = self.ENG_SEM[eng]
        deps = []
        for n in rd:
            t = self._wtok.get(n)
            if t is not None:
                deps.append(t)
        for n in wr:
            t = self._wtok.get(n)
            if t is not None and t[0] != own:
                deps.append(t)
            for oe, v in self._rtok.get(n, {}).items():
                deps.append((oe, v))
        deps.extend(x for x in extra if x is not None)
        for t in deps:
            if t[0] == own and eng == "t":
                continue
            self._wait(eng, t)
        if eng in "qg":
            s = dsem or own
            self.cnt[s] += 16
            val = self.cnt[s]
            sem = self.sem[s]
            self.stream[eng].append(
                lambda E, fn=fn, sem=sem: fn(E).then_inc(sem, 16))
            if dsem is None:
                self._wait(eng, (s, val))
            tok = (s, val)
        elif inc:
            self.cnt[own] += 1
            val = self.cnt[own]
            sem = self.sem[own]
            self.stream[eng].append(
                lambda E, fn=fn, sem=sem: fn(E).then_inc(sem, 1))
            tok = (own, val)
        else:
            self.stream[eng].append(lambda E, fn=fn: fn(E))
            tok = (own, self.cnt[own])
        for n in wr:
            self._wtok[n] = tok
            self._rtok[n] = {}
        for n in rd:
            self._rtok.setdefault(n, {})[tok[0]] = tok[1]
        return tok

    def barrier(self):
        snap = dict(self.cnt)
        for eng in "tvsqg":
            for s, v in snap.items():
                if s != self.ENG_SEM[eng]:
                    self._wait(eng, (s, v))

    def lp(self, fn):
        def g(E, fn=fn):
            with self.nc.allow_low_precision(reason="bf16 operand for following matmul"):
                return fn(E)
        return g

    def bank(self, i):
        return self.pf[:, i * 512:(i + 1) * 512]

    def tbank(self, i):
        return self.pb[:, i * 1024:(i + 1) * 1024]

    def _init_phase(self):
        v = "v"
        self.op(v, lambda E: E.memset(self.consts[:, 0:1], -30.0), wr=("consts",))
        self.op(v, lambda E: E.memset(self.consts[:, 1:2], 1e-5), wr=("consts",))
        self.op(v, lambda E: E.memset(self.ones[:], 1.0), wr=("ones",))
        vr = self.vtm[:].rearrange("p (t h c) -> p t h c", h=NH, c=65)
        self.op(v, lambda E: E.memset(vr[:, :, :, 64:65], 1.0), wr=("vtm",))
        self.op("g", lambda E: E.dma_start(self.ident[:], self.identd), wr=("ident",))
        self.op("g", lambda E: E.dma_start(self.cmask[:], self.cmaskd), wr=("cmask",))
        self.op("g", lambda E: E.dma_start(
            self.rT[:].rearrange("p (c e) -> p c e", e=8),
            self.rTd.rearrange("(c p) e -> p c e", p=128)), wr=("rT",))
        self.op("g", lambda E: E.dma_start(
            self.rTf[:].rearrange("p (c e) -> p c e", e=8),
            self.rTd.rearrange("(c p) e -> p c e", p=128)), wr=("rTf",))
        self.op("v", self.lp(lambda E: E.tensor_tensor(
            self.rTlo[:], self.rTf[:], self.rT[:], ALU.subtract)),
            rd=("rTf", "rT"), wr=("rTlo",))
        self.op("g", lambda E: E.dma_start(
            self.wo[:].rearrange("p (j n) -> p j n", j=8), self.woTp), wr=("wo",))

    def _precast(self):
        for e in range(NE):
            gT_e = self.gTd[e].rearrange("(c p) i -> p c i", p=128)
            uT_e = self.uTd[e].rearrange("(c p) i -> p c i", p=128)
            for grp in range(WG):
                par = grp % 2
                sg, sd = self.stg_gu[par], self.stg_dn[par]
                ng, nd = f"stg_gu{par}", f"stg_dn{par}"
                for sub in range(2):
                    ic = grp * 2 + sub
                    i0 = ic * 128
                    self.op("g", lambda E, sg=sg, s=sub, gT_e=gT_e, i0=i0: E.dma_start(
                        sg[:, s * 2048:s * 2048 + 1024].rearrange("p (c i) -> p c i", c=8),
                        gT_e[:, :, i0:i0 + 128]), wr=(ng,), dsem=f"sg{par}")
                    self.op("g", lambda E, sg=sg, s=sub, uT_e=uT_e, i0=i0: E.dma_start(
                        sg[:, s * 2048 + 1024:s * 2048 + 2048].rearrange("p (c i) -> p c i", c=8),
                        uT_e[:, :, i0:i0 + 128]), wr=(ng,), dsem=f"sg{par}")
                    self.op("g", lambda E, sd=sd, s=sub, e=e, i0=i0: E.dma_start(
                        sd[:, s * 1024:s * 1024 + 1024],
                        self.dTd[e, i0:i0 + 128, :]), wr=(nd,), dsem=f"sg{par}")
                self.op("q", lambda E, e=e, grp=grp, sg=sg: E.dma_start(
                    self.wgu[e, grp], sg[:]), rd=(ng,), wr=(f"wgu{e}_{grp}",),
                    dsem=f"wb{par}")
                self.op("q", lambda E, e=e, grp=grp, sd=sd: E.dma_start(
                    self.wdn[e, grp], sd[:]), rd=(nd,), wr=(f"wdn{e}_{grp}",),
                    dsem=f"wb{par}")

    def _a1(self, b):
        nc = self.nc
        XCN = tuple(f"pts{t}" for t in range(8)) + ("xcast",)
        for g in range(NSTRIP):
            src = self.xin[b, g * 512:(g + 1) * 512, :].rearrange("(t p) d -> p t d", p=128)
            self.op("g", lambda E, src=src: E.dma_start(
                self.xcast[:].rearrange("p (t d) -> p t d", t=4), src), wr=XCN)
            for c in range(NCH):
                tb = (g * NCH + c) % 2
                tname = f"T{tb}"
                for t in range(4):
                    self.op("t", lambda E, tb=tb, t=t, c=c: E.transpose(
                        self.tbank(tb)[:, t * 128:(t + 1) * 128],
                        self.xcast[:, t * 1024 + c * 128:t * 1024 + c * 128 + 128],
                        self.ident[:]),
                        rd=XCN + ("ident",), wr=(tname,), inc=(t == 3))
                self.op("v", lambda E, tb=tb, c=c, g=g: E.tensor_copy(
                    self.xfm[:, c * SEQ + g * 512:c * SEQ + g * 512 + 512],
                    self.tbank(tb)[:, 0:512]), rd=(tname,), wr=("xfm",))
        for wname, dst, dstn, wsrc in (("wq_q", self.qfm, "qfm", self.wqT),
                                       ("wq_k", self.kfm, "kfm", self.wkT)):
            self.op("g", lambda E, wsrc=wsrc: E.dma_start(
                self.wq[:].rearrange("p (c n) -> p c n", c=8),
                wsrc.rearrange("(c p) n -> p c n", p=128)), wr=("wq",))
            for c in range(NCH):
                for d in range(NCH):
                    for g in range(NSTRIP):
                        self.op("t", lambda E, g=g, d=d, c=c: E.matmul(
                            self.bank(g),
                            self.wq[:, d * 1024 + c * 128:d * 1024 + c * 128 + 128],
                            self.xfm[:, d * SEQ + g * 512:d * SEQ + g * 512 + 512],
                            start=(d == 0), stop=(d == NCH - 1), skip_group_check=True),
                            rd=("wq", "xfm"), wr=(f"B{g}",),
                            inc=(d == NCH - 1))
                for g in range(NSTRIP):
                    self.op("v", lambda E, g=g, c=c, dst=dst: E.tensor_copy(
                        dst[:, c * SEQ + g * 512:c * SEQ + g * 512 + 512],
                        self.bank(g)), rd=(f"B{g}",), wr=(dstn,))
        self.op("g", lambda E: E.dma_start(
            self.wq[:].rearrange("p (c n) -> p c n", c=8),
            self.wvT.rearrange("(c p) n -> p c n", p=128)), wr=("wq",))
        vr = self.vtm[:].rearrange("p (t h c) -> p t h c", h=NH, c=65)
        for t in range(NTILE):
            pair = (4, 5) if t % 2 == 0 else (2, 3)
            for d in range(NCH):
                for hlf in range(2):
                    self.op("t", lambda E, pair=pair, hlf=hlf, d=d, t=t: E.matmul(
                        self.bank(pair[hlf]),
                        self.xfm[:, d * SEQ + t * 128:d * SEQ + t * 128 + 128],
                        self.wq[:, d * 1024 + hlf * 512:d * 1024 + hlf * 512 + 512],
                        start=(d == 0), stop=(d == NCH - 1), skip_group_check=True),
                        rd=("wq", "xfm"), wr=(f"B{pair[hlf]}",),
                        inc=(d == NCH - 1))
            for hlf in range(2):
                self.op("v", lambda E, pair=pair, hlf=hlf, t=t, vr=vr: E.tensor_copy(
                    vr[:, t, hlf * 8:hlf * 8 + 8, 0:64],
                    self.bank(pair[hlf]).rearrange("p (h c) -> p h c", c=64)),
                    rd=(f"B{pair[hlf]}",), wr=("vtm",))
        if "qkv" in self.taps and b == 0:
            self.op("q", lambda E: E.dma_start(self.tapd["tq"], self.qfm[:]), rd=("qfm",))
            self.op("q", lambda E: E.dma_start(self.tapd["tk"], self.kfm[:]), rd=("kfm",))
            tok = self.op("q", lambda E: E.dma_start(self.tapd["tv"], self.vtm[:]), rd=("vtm",))
            self.out_toks.append(tok)

    def _attn(self, b):
        cneg = self.consts[:, 0:1]
        ceps = self.consts[:, 1:2]
        for g in range(NSTRIP):
            src = self.xin[b, g * 512:(g + 1) * 512, :].rearrange("(t p) d -> p t d", p=128)
            self.op("q", lambda E, src=src: E.dma_start(
                self.xres[:].rearrange("p (t d) -> p t d", t=4), src), wr=("xres",))
            T = 4 * g + 4

            def opr_tile(h, t):
                self.op("t", lambda E, t=t, h=h: E.matmul(
                    self.bank(2)[0:65, :],
                    self.vtm[:, t * 1040 + h * 65:t * 1040 + h * 65 + 65],
                    self.pts[:, t * 512:(t + 1) * 512],
                    start=(t == 0), stop=(t == T - 1), skip_group_check=True),
                    rd=("vtm", f"pts{t}"), wr=("B2",))

            def head_scores(h):
                hr = (h % 2) * 64
                ch = h // 2
                for t0 in range(0, T, 2):
                    qd = (t0 // 2) % 2
                    bks = (0, 1) if qd == 0 else (4, 5)
                    for j, t in enumerate((t0, t0 + 1)):
                        self.op("t", lambda E, bk=bks[j], hr=hr, ch=ch, t=t, g=g: E.matmul(
                            self.bank(bk),
                            self.kfm[hr:hr + 64, ch * SEQ + t * 128:ch * SEQ + t * 128 + 128],
                            self.qfm[hr:hr + 64, ch * SEQ + g * 512:ch * SEQ + g * 512 + 512],
                            start=True, stop=True, tile_position=(hr, 0),
                            skip_group_check=True),
                            rd=("kfm", "qfm"), wr=(f"B{bks[j]}",))
                    off = 0 if qd == 0 else 2048
                    self.op("s", lambda E, t0=t0, off=off: E.activation(
                        self.pts[:, t0 * 512:(t0 + 2) * 512],
                        self.pf[:, off:off + 1024],
                        AF.Exp, bias=cneg, scale=0.125),
                        rd=(f"B{bks[0]}", f"B{bks[1]}", "consts"),
                        wr=(f"pts{t0}", f"pts{t0 + 1}"))
                    for t in (t0, t0 + 1):
                        r = t - 4 * g
                        if r >= 0:
                            self.op("v", lambda E, t=t, r=r: E.tensor_tensor(
                                self.pts[:, t * 512:(t + 1) * 512],
                                self.pts[:, t * 512:(t + 1) * 512],
                                self.cmask[:, r * 512:(r + 1) * 512], ALU.mult),
                                rd=(f"pts{t}", "cmask"), wr=(f"pts{t}",))

            def head_opr(h):
                for t in range(T):
                    opr_tile(h, t)
                self.op("s", lambda E: E.activation(
                    self.otmpf[0:64, :], self.bank(2)[0:64, :], AF.Identity),
                    rd=("B2",), wr=("otmpf",))
                self.op("s", lambda E: E.activation(
                    self.tmerge[64:65, 0:512], self.bank(2)[64:65, :], AF.Identity),
                    rd=("B2", "tmerge"), wr=("tmerge",))
                self.op("v", self.lp(lambda E: E.reciprocal(
                    self.recipb[64:65, 0:512], self.tmerge[64:65, 0:512])),
                    rd=("tmerge",), wr=("recipb",))

            def head_bcast(h):
                self.op("t", lambda E: E.matmul(
                    self.bank(3), self.ones[64:65, 0:128], self.recipb[64:65, 0:512],
                    start=True, stop=True, tile_position=(64, 0), skip_group_check=True),
                    rd=("ones", "recipb"), wr=("B3",))

            def head_mul(h):
                if h < 8:
                    self.op("v", lambda E, h=h: E.tensor_tensor(
                        self.ofm[0:64, h * 512:(h + 1) * 512],
                        self.otmpf[0:64, :], self.bank(3)[0:64, :], ALU.mult),
                        rd=("otmpf", "B3"), wr=("ofm",))
                else:
                    self.op("v", lambda E: E.tensor_tensor(
                        self.otmp2[0:64, :], self.otmpf[0:64, :],
                        self.bank(3)[0:64, :], ALU.mult),
                        rd=("otmpf", "B3"), wr=("otmp2",))

            def head_shift(h):
                if h >= 8:
                    self.op("t", lambda E: E.matmul(
                        self.bank(3)[64:128, :], self.ident[0:64, 0:64],
                        self.otmp2[0:64, 0:512], start=True, stop=True,
                        tile_position=(0, 64), skip_group_check=True),
                        rd=("ident", "otmp2"), wr=("B3",))
                    self.op("s", lambda E, h=h: E.activation(
                        self.ofm[64:128, (h - 8) * 512:(h - 7) * 512],
                        self.bank(3)[64:128, :], AF.Identity), rd=("B3",), wr=("ofm",))

            for h in range(NH):
                head_scores(h)
                if h >= 1:
                    head_bcast(h - 1)
                    head_mul(h - 1)
                head_opr(h)
                if h >= 1:
                    head_shift(h - 1)
            head_bcast(NH - 1)
            head_mul(NH - 1)
            head_shift(NH - 1)
            if "ofm" in self.taps and b == 0 and g == 0:
                tok = self.op("q", lambda E: E.dma_start(self.tapd["tofm"], self.ofm[:]), rd=("ofm",))
                self.out_toks.append(tok)
            for qt in range(4):
                for j in range(NCH):
                    for hlf in range(2):
                        bk0, bk1 = (2, 3) if hlf == 0 else (4, 5)
                        self.op("t", lambda E, j=j, hlf=hlf, qt=qt, bk0=bk0: E.matmul(
                            self.bank(bk0),
                            self.ofm[0:64, j * 512 + qt * 128:j * 512 + qt * 128 + 128],
                            self.wo[0:64, j * 1024 + hlf * 512:j * 1024 + hlf * 512 + 512],
                            start=(j == 0), stop=(j == NCH - 1),
                            tile_position=(0, 0), skip_group_check=True),
                            rd=("ofm", "wo"), wr=(f"B{bk0}",), inc=(j == NCH - 1))
                        self.op("t", lambda E, j=j, hlf=hlf, qt=qt, bk1=bk1: E.matmul(
                            self.bank(bk1),
                            self.ofm[64:128, j * 512 + qt * 128:j * 512 + qt * 128 + 128],
                            self.wo[64:128, j * 1024 + hlf * 512:j * 1024 + hlf * 512 + 512],
                            start=(j == 0), stop=(j == NCH - 1),
                            tile_position=(64, 0), skip_group_check=True),
                            rd=("ofm", "wo"), wr=(f"B{bk1}",), inc=(j == NCH - 1))
                xt = self.xres[:, qt * 1024:(qt + 1) * 1024]
                self.op("v", lambda E, xt=xt: E.tensor_tensor(
                    self.tmerge[:, 0:512], xt[:, 0:512], self.bank(2), ALU.add),
                    rd=("B2", "xres"), wr=("tmerge",))
                self.op("v", lambda E: E.tensor_tensor(
                    self.tmerge[:, 0:512], self.tmerge[:, 0:512], self.bank(3), ALU.add),
                    rd=("B3", "tmerge"), wr=("tmerge",))
                self.op("v", lambda E, xt=xt: E.tensor_tensor(
                    self.tmerge[:, 512:1024], xt[:, 512:1024], self.bank(4), ALU.add),
                    rd=("B4", "xres"), wr=("tmerge",))
                self.op("v", lambda E: E.tensor_tensor(
                    self.tmerge[:, 512:1024], self.tmerge[:, 512:1024], self.bank(5), ALU.add),
                    rd=("B5", "tmerge"), wr=("tmerge",))
                self.op("v", lambda E, xt=xt, qt=qt: E.tensor_tensor_reduce(
                    out=xt, in0=self.tmerge[:], in1=self.tmerge[:], scale=0.5,
                    scalar=0.0, op0=ALU.add, op1=ALU.add,
                    accum_out=self.stats[:, qt:qt + 1]),
                    rd=("tmerge",), wr=("xres", "stats"))
                self.op("s", lambda E, xt=xt, qt=qt: E.activation(
                    self.tmerge[:], xt, AF.Square,
                    accum_out=self.stats[:, 4 + qt:5 + qt]),
                    rd=("xres",), wr=("tmerge", "stats"))
            self._ln(self.stats, ceps)
            xfm_r = self.xfm[:].rearrange("p (c s) -> p c s", s=SEQ)
            for qt in range(4):
                xt = self.xres[:, qt * 1024:(qt + 1) * 1024]
                tn = (f"pts{2 * qt}", f"pts{2 * qt + 1}", "xcast")
                self.op("s", lambda E, xt=xt, qt=qt: E.activation(
                    xt, xt, AF.Identity, bias=self.stats[:, 12 + qt:13 + qt],
                    scale=self.stats[:, 24 + qt:25 + qt]),
                    rd=("stats", "xres"), wr=("xres",))
                self.op("v", lambda E, xt=xt, qt=qt: E.tensor_copy(
                    self.xcast[:, qt * 1024:(qt + 1) * 1024], xt),
                    rd=("xres",), wr=tn)
                for c in range(NCH):
                    tb = c // 4
                    self.op("t", lambda E, tb=tb, qt=qt, c=c: E.transpose(
                        self.tbank(tb)[:, (c % 4) * 128:(c % 4) * 128 + 128],
                        self.xcast[:, qt * 1024 + c * 128:qt * 1024 + c * 128 + 128],
                        self.ident[:]),
                        rd=tn + ("ident",), wr=(f"T{tb}",), inc=(c % 4 == 3))
                for tb in range(2):
                    self.op("v", lambda E, tb=tb, qt=qt, g=g, xfm_r=xfm_r: E.tensor_copy(
                        xfm_r[:, tb * 4:tb * 4 + 4, g * 512 + qt * 128:g * 512 + qt * 128 + 128],
                        self.tbank(tb)[:, 0:512].rearrange("p (c k) -> p c k", k=128)),
                        rd=(f"T{tb}",), wr=("xfm",))
            dst = self.x1scr[g * 512:(g + 1) * 512, :].rearrange("(t p) d -> p t d", p=128)
            tok = self.op("q", lambda E, dst=dst: E.dma_start(
                dst, self.xres[:].rearrange("p (t d) -> p t d", t=4)),
                rd=("xres",), wr=(f"x1s{g}",))
            if "x1" in self.taps and b == 0:
                dst2 = self.tapd["tx1"][g * 512:(g + 1) * 512, :].rearrange("(t p) d -> p t d", p=128)
                tok = self.op("q", lambda E, dst2=dst2: E.dma_start(
                    dst2, self.xres[:].rearrange("p (t d) -> p t d", t=4)), rd=("xres",))
                self.out_toks.append(tok)
            XCN = tuple(f"pts{t}" for t in range(8)) + ("xcast",)
            self.op("v", self.lp(lambda E: E.tensor_tensor(
                self.xcast[:], self.xres[:], self.xcast[:], ALU.subtract)),
                rd=("xres",) + XCN, wr=XCN)
            for c in range(NCH):
                tb = c % 2
                for t in range(4):
                    self.op("t", lambda E, tb=tb, t=t, c=c: E.transpose(
                        self.tbank(tb)[:, t * 128:(t + 1) * 128],
                        self.xcast[:, t * 1024 + c * 128:t * 1024 + c * 128 + 128],
                        self.ident[:]),
                        rd=XCN + ("ident",), wr=(f"T{tb}",), inc=(t == 3))
                self.op("v", lambda E, tb=tb, c=c: E.tensor_copy(
                    self.x1lofm[:, c * 512:(c + 1) * 512],
                    self.tbank(tb)[:, 0:512]), rd=(f"T{tb}",), wr=("x1lofm", "ofm"))
            if "lof" in self.taps and b == 0 and g == 0:
                tok = self.op("q", lambda E: E.dma_start(self.tapd["tlof"], self.x1lofm[:]),
                              rd=("x1lofm", "ofm"))
                self.out_toks.append(tok)
            for qt in range(4):
                Q = g * 4 + qt
                nmm = 3 * NCH
                i = 0
                for lhs_sel, rhs_sel in (("hi", "rT"), ("hi", "rTlo"), ("lo", "rT")):
                    for c in range(NCH):
                        if lhs_sel == "hi":
                            lap = self.xfm[:, c * SEQ + Q * 128:c * SEQ + Q * 128 + 128]
                            lrd = "xfm"
                        else:
                            lap = self.x1lofm[:, c * 512 + qt * 128:c * 512 + qt * 128 + 128]
                            lrd = "ofm"
                        rap = (self.rT if rhs_sel == "rT" else self.rTlo)[:, c * 8:c * 8 + 8]
                        i += 1
                        self.op("t", lambda E, qt=qt, lap=lap, rap=rap, i=i, nmm=nmm: E.matmul(
                            self.bank(0)[:, qt * 8:qt * 8 + 8], lap, rap,
                            start=(i == 1), stop=(i == nmm), skip_group_check=True),
                            rd=(lrd, rhs_sel), wr=("B0",), inc=(i == nmm))
                self.op("v", lambda E, qt=qt: E.reduce_max(
                    self.stats[:, 28 + qt:29 + qt], self.bank(0)[:, qt * 8:qt * 8 + 8],
                    axis=AX.X), rd=("B0",), wr=("stats",))
                self.op("v", lambda E, qt=qt, Q=Q: E.tensor_scalar(
                    self.masktm[:, Q * 8:Q * 8 + 8], self.bank(0)[:, qt * 8:qt * 8 + 8],
                    self.stats[:, 28 + qt:29 + qt], None, ALU.is_ge),
                    rd=("B0", "stats"), wr=("masktm",))
        if "mask" in self.taps and b == 0:
            tok = self.op("q", lambda E: E.dma_start(self.tapd["tmask"], self.masktm[:]),
                          rd=("masktm",))
            self.out_toks.append(tok)

    def _ln(self, stats, ceps):
        self.op("v", lambda E: E.tensor_scalar_mul(
            stats[:, 8:12], stats[:, 0:4], -1.0 / DM), rd=("stats",), wr=("stats",))
        self.op("v", lambda E: E.tensor_tensor(
            stats[:, 12:16], stats[:, 8:12], stats[:, 8:12], ALU.mult),
            rd=("stats",), wr=("stats",))
        self.op("v", lambda E: E.scalar_tensor_tensor(
            stats[:, 16:20], stats[:, 4:8], 1.0 / DM, stats[:, 12:16],
            ALU.mult, ALU.subtract), rd=("stats",), wr=("stats",))
        self.op("s", lambda E: E.activation(
            stats[:, 20:24], stats[:, 16:20], AF.Sqrt, bias=ceps),
            rd=("stats", "consts"), wr=("stats",))
        self.op("v", lambda E: E.reciprocal(
            stats[:, 24:28], stats[:, 20:24]), rd=("stats",), wr=("stats",))
        self.op("v", lambda E: E.tensor_tensor(
            stats[:, 12:16], stats[:, 8:12], stats[:, 24:28], ALU.mult),
            rd=("stats",), wr=("stats",))

    def _ffn(self, b):
        ceps = self.consts[:, 1:2]
        self._wait("q", ("t", self.cnt["t"]))
        self._wait("q", ("v", self.cnt["v"]))
        def reload(g):
            src = self.x1scr[g * 512:(g + 1) * 512, :].rearrange("(t p) d -> p t d", p=128)
            self.op("q", lambda E, src=src: E.dma_start(
                self.xres[:].rearrange("p (t d) -> p t d", t=4), src),
                rd=(f"x1s{g}",), wr=("xres",))

        reload(0)
        for g in range(NSTRIP):
            def prep(e, g=g):
                for qt in range(4):
                    Q = g * 4 + qt
                    self.op("t", lambda E, qt=qt, Q=Q, e=e: E.transpose(
                        self.tbank(0)[0:1, qt * 128:qt * 128 + 128],
                        self.masktm[:, Q * 8 + e:Q * 8 + e + 1], self.ident[:]),
                        rd=("masktm", "ident"), wr=("T0",), inc=(qt == 3))
                self.op("v", lambda E: E.tensor_copy(
                    self.mrow[0:1, 0:512], self.tbank(0)[0:1, 0:512]),
                    rd=("T0",), wr=("mrow",))
                self.op("t", lambda E: E.matmul(
                    self.bank(0), self.ones[0:1, 0:128], self.mrow[0:1, 0:512],
                    start=True, stop=True, skip_group_check=True),
                    rd=("ones", "mrow"), wr=("B0",))
                for c in range(NCH):
                    self.op("v", lambda E, c=c, g=g: E.tensor_tensor(
                        self.xefm[:, c * 512:(c + 1) * 512],
                        self.xfm[:, c * SEQ + g * 512:c * SEQ + g * 512 + 512],
                        self.bank(0), ALU.mult), rd=("xfm", "B0"), wr=("xefm",))

            if g == 0:
                prep(0)
            for e in range(NE):
                for ic in range(ICN):
                    par = ic % 4
                    wb = self.wbufs[par]
                    wn = f"wbuf{par}"
                    wsl = self.wgu[e, ic // 2][:, (ic % 2) * 2048:(ic % 2) * 2048 + 2048]
                    self.op("q" if par % 2 == 0 else "g",
                        lambda E, wb=wb, wsl=wsl: E.dma_start(wb[:], wsl),
                        rd=(f"wgu{e}_{ic // 2}",), wr=(wn,), dsem=f"w{par}")
                    gbk, ubk = (0, 1) if ic % 2 == 0 else (2, 3)
                    for d in range(NCH):
                        self.op("t", lambda E, wb=wb, d=d, gbk=gbk: E.matmul(
                            self.bank(gbk),
                            wb[:, d * 128:d * 128 + 128],
                            self.xefm[:, d * 512:(d + 1) * 512],
                            start=(d == 0), stop=(d == NCH - 1), skip_group_check=True),
                            rd=(wn, "xefm"), wr=(f"B{gbk}",), inc=(d == NCH - 1))
                    for d in range(NCH):
                        self.op("t", lambda E, wb=wb, d=d, ubk=ubk: E.matmul(
                            self.bank(ubk),
                            wb[:, 1024 + d * 128:1024 + d * 128 + 128],
                            self.xefm[:, d * 512:(d + 1) * 512],
                            start=(d == 0), stop=(d == NCH - 1), skip_group_check=True),
                            rd=(wn, "xefm"), wr=(f"B{ubk}",), inc=(d == NCH - 1))
                    sp = ic % 2
                    self.op("s", lambda E, gbk=gbk, sp=sp: E.activation(
                        self.sgl[:, sp * 512:(sp + 1) * 512], self.bank(gbk), AF.Silu),
                        rd=(f"B{gbk}",), wr=(f"sgl{sp}",))
                    self.op("v", lambda E, ic=ic, sp=sp, ubk=ubk: E.tensor_tensor(
                        self.zfm[:, ic * 512:(ic + 1) * 512],
                        self.sgl[:, sp * 512:(sp + 1) * 512],
                        self.bank(ubk), ALU.mult),
                        rd=(f"sgl{sp}", f"B{ubk}"), wr=("zfm",))
                if e + 1 < NE:
                    prep(e + 1)
                elif g + 1 < NSTRIP:
                    prep(0, g + 1)
                for hlf in range(2):
                    for grp in range(WG):
                        par = (hlf * WG + grp) % 4
                        db = self.dnbuf[par]
                        dn = f"dnbuf{par}"
                        wsl = self.wdn[e, grp].rearrange(
                            "p (s c) -> p s c", s=2)[:, :, hlf * 512:hlf * 512 + 512]
                        self.op("q" if par == 0 else "g",
                            lambda E, db=db, wsl=wsl: E.dma_start(
                                db[:, 0:1024].rearrange("p (s c) -> p s c", s=2), wsl),
                            rd=(f"wdn{e}_{grp}",), wr=(dn,), dsem=f"d{par}")
                        for sub in range(2):
                            ic = grp * 2 + sub
                            for tt in range(4):
                                bk = 2 + tt
                                self.op("t", lambda E, ic=ic, tt=tt, db=db,
                                        sub=sub, bk=bk: E.matmul(
                                    self.bank(bk),
                                    self.zfm[:, ic * 512 + tt * 128:ic * 512 + tt * 128 + 128],
                                    db[:, sub * 512:sub * 512 + 512],
                                    start=(ic == 0), stop=(ic == ICN - 1),
                                    skip_group_check=True),
                                    rd=(dn, "zfm"), wr=(f"B{bk}",))
                    for tt in range(4):
                        bk = 2 + tt
                        dstc = tt * 1024 + hlf * 512
                        if e == 0:
                            self.op("v", lambda E, bk=bk, dstc=dstc: E.tensor_copy(
                                self.yacc[:, dstc:dstc + 512], self.bank(bk)),
                                rd=(f"B{bk}",), wr=("yacc",))
                        else:
                            self.op("v", lambda E, bk=bk, dstc=dstc: E.tensor_tensor(
                                self.yacc[:, dstc:dstc + 512],
                                self.yacc[:, dstc:dstc + 512],
                                self.bank(bk), ALU.add),
                                rd=(f"B{bk}", "yacc"), wr=("yacc",))
            for qt in range(4):
                xt = self.xres[:, qt * 1024:(qt + 1) * 1024]
                self.op("v", lambda E, xt=xt, qt=qt: E.tensor_tensor_reduce(
                    out=xt, in0=xt, in1=self.yacc[:, qt * 1024:(qt + 1) * 1024],
                    scale=1.0, scalar=0.0, op0=ALU.add, op1=ALU.add,
                    accum_out=self.stats[:, qt:qt + 1]),
                    rd=("yacc", "xres"), wr=("xres", "stats"))
                self.op("s", lambda E, xt=xt, qt=qt: E.activation(
                    self.tmerge[:], xt, AF.Square,
                    accum_out=self.stats[:, 4 + qt:5 + qt]),
                    rd=("xres",), wr=("tmerge", "stats"))
            self._ln(self.stats, ceps)
            for qt in range(4):
                xt = self.xres[:, qt * 1024:(qt + 1) * 1024]
                self.op("s", lambda E, xt=xt, qt=qt: E.activation(
                    xt, xt, AF.Identity, bias=self.stats[:, 12 + qt:13 + qt],
                    scale=self.stats[:, 24 + qt:25 + qt]),
                    rd=("stats", "xres"), wr=("xres",))
            dst = self.outd[b, g * 512:(g + 1) * 512, :].rearrange("(t p) d -> p t d", p=128)
            tok = self.op("q", lambda E, dst=dst: E.dma_start(
                dst, self.xres[:].rearrange("p (t d) -> p t d", t=4)), rd=("xres",))
            self.out_toks.append(tok)
            if g + 1 < NSTRIP:
                reload(g + 1)

    def _finish(self):
        for tok in self.out_toks:
            self._wait("q", tok)
        self.barrier()


def _host_prep(L0_attn_X, L0_attn_q_proj, L0_attn_k_proj, L0_attn_v_proj,
               L0_attn_o_proj, L0_ffn_router_w, L0_ffn_gate_w, L0_ffn_up_w,
               L0_ffn_down_w):
    X = np.ascontiguousarray(np.asarray(L0_attn_X, dtype=np.float32))
    wqT = np.ascontiguousarray(np.asarray(L0_attn_q_proj, dtype=np.float32).T)
    wkT = np.ascontiguousarray(np.asarray(L0_attn_k_proj, dtype=np.float32).T)
    wvT = np.ascontiguousarray(np.asarray(L0_attn_v_proj, dtype=np.float32).T)
    woT = np.asarray(L0_attn_o_proj, dtype=np.float32).T
    w16 = woT.reshape(16, 64, DM)
    woTp = np.ascontiguousarray(np.concatenate(
        [w16[0:8].transpose(1, 0, 2), w16[8:16].transpose(1, 0, 2)], axis=0))
    rT = np.ascontiguousarray(np.asarray(L0_ffn_router_w, dtype=np.float32).T)
    gT = np.ascontiguousarray(np.asarray(L0_ffn_gate_w, dtype=np.float32).transpose(0, 2, 1))
    uT = np.ascontiguousarray(np.asarray(L0_ffn_up_w, dtype=np.float32).transpose(0, 2, 1))
    dT = np.ascontiguousarray(np.asarray(L0_ffn_down_w, dtype=np.float32).transpose(0, 2, 1))
    ident = np.eye(128, dtype=BFNP)
    cm = np.zeros((128, 4 * 512), dtype=BFNP)
    qq = np.arange(512)[None, :]
    ss = np.arange(128)[:, None]
    for r in range(4):
        cm[:, r * 512:(r + 1) * 512] = (qq >= r * 128 + ss).astype(BFNP)
    shared = {"wqT": wqT, "wkT": wkT, "wvT": wvT, "woTp": woTp, "rTd": rT,
              "gTd": gT, "uTd": uT, "dTd": dT, "identd": ident, "cmaskd": cm}
    return X, shared


def kernel(**inputs):
    X, shared = _host_prep(**inputs)
    mk = MK(nseq=4)
    in_maps = []
    for c in range(NCORES):
        m = dict(shared)
        m["xin"] = np.ascontiguousarray(X[c * 4:(c + 1) * 4])
        in_maps.append(m)
    res = run_bass_kernel_spmd(mk.nc, in_maps, core_ids=list(range(NCORES)))
    return np.concatenate([res.results[c]["outd"] for c in range(NCORES)], axis=0)


# version 13
# speedup vs baseline: 1.0088x; 1.0088x over previous
import numpy as np
import ml_dtypes
import concourse.bass as bass
import concourse.mybir as mybir
from concourse.bass_utils import run_bass_kernel_spmd
from concourse.library_overlay import lower_extended_insts

F32 = mybir.dt.float32
BF16 = mybir.dt.bfloat16
AF = mybir.ActivationFunctionType
ALU = mybir.AluOpType
AX = mybir.AxisListType
BFNP = ml_dtypes.bfloat16

NCORES = 8
SEQ, DM, NH, DK = 2048, 1024, 16, 64
NSTRIP, STRIP = 4, 512
NTILE = SEQ // 128
NCH = DM // 128
FI, ICN = 2816, 22
NE = 8
WG = 11

KB = 1024


class MK:
    def __init__(self, nseq=4, taps=()):
        self.nseq = nseq
        self.taps = set(taps)
        nc = bass.Bass("TRN2", target_bir_lowering=False)
        self.nc = nc
        self.xin = nc.dram_tensor("xin", [nseq, SEQ, DM], F32, kind="ExternalInput").ap()
        self.wqT = nc.dram_tensor("wqT", [DM, DM], F32, kind="ExternalInput").ap()
        self.wkT = nc.dram_tensor("wkT", [DM, DM], F32, kind="ExternalInput").ap()
        self.wvT = nc.dram_tensor("wvT", [DM, DM], F32, kind="ExternalInput").ap()
        self.woTp = nc.dram_tensor("woTp", [128, 8, DM], F32, kind="ExternalInput").ap()
        self.rTd = nc.dram_tensor("rTd", [DM, 8], F32, kind="ExternalInput").ap()
        self.gTd = nc.dram_tensor("gTd", [NE, DM, FI], F32, kind="ExternalInput").ap()
        self.uTd = nc.dram_tensor("uTd", [NE, DM, FI], F32, kind="ExternalInput").ap()
        self.dTd = nc.dram_tensor("dTd", [NE, FI, DM], F32, kind="ExternalInput").ap()
        self.identd = nc.dram_tensor("identd", [128, 128], BF16, kind="ExternalInput").ap()
        self.cmaskd = nc.dram_tensor("cmaskd", [128, 4 * 512], BF16, kind="ExternalInput").ap()
        self.outd = nc.dram_tensor("outd", [nseq, SEQ, DM], F32, kind="ExternalOutput").ap()
        self.x1scr = nc.dram_tensor("x1scr", [SEQ, DM], F32).ap()
        self.wgu = nc.dram_tensor("wgu", [NE, WG, 128, 4096], BF16).ap()
        self.wdn = nc.dram_tensor("wdn", [NE, WG, 128, 2048], BF16).ap()
        self.tapd = {}
        if "qkv" in self.taps:
            self.tapd["tq"] = nc.dram_tensor("tq", [128, NCH * SEQ], BF16, kind="ExternalOutput").ap()
            self.tapd["tk"] = nc.dram_tensor("tk", [128, NCH * SEQ], BF16, kind="ExternalOutput").ap()
            self.tapd["tv"] = nc.dram_tensor("tv", [128, NTILE * NH * 65], BF16, kind="ExternalOutput").ap()
        if "x1" in self.taps:
            self.tapd["tx1"] = nc.dram_tensor("tx1", [SEQ, DM], F32, kind="ExternalOutput").ap()
        if "mask" in self.taps:
            self.tapd["tmask"] = nc.dram_tensor("tmask", [128, NTILE * 8], BF16, kind="ExternalOutput").ap()
        if "lof" in self.taps:
            self.tapd["tlof"] = nc.dram_tensor("tlof", [128, NCH * 512], BF16, kind="ExternalOutput").ap()
        if "ofm" in self.taps:
            self.tapd["tofm"] = nc.dram_tensor("tofm", [128, 8 * 512], BF16, kind="ExternalOutput").ap()

        self.stream = {e: [] for e in "tvsqg"}
        self.hwm = {e: {} for e in "tvsqg"}
        self._wtok = {}
        self._rtok = {}
        self.out_toks = []

        self._ctx = []

        def E(cm):
            self._ctx.append(cm)
            return cm.__enter__()

        self._cleanup = nc.cleanup_on_exit()
        self._cleanup.__enter__()

        self.sem = {}
        for e in ["t", "v", "s", "q", "g", "w0", "w1", "w2", "w3",
                  "d0", "d1", "d2", "d3", "wb0", "wb1", "sg0", "sg1"]:
            self.sem[e] = E(nc.semaphore("sem_" + e))
        self.cnt = {e: 0 for e in self.sem}

        total = 212832 - 64
        self.arena_h = E(nc.sbuf_tensor("arena", [128, total], mybir.dt.uint8))
        self.base = nc.lookup_mloc(self.arena_h).addr
        self._off = 0
        self._sizes = {}

        def sb(name, cols, dtype, off=None):
            bsz = cols * (2 if dtype == BF16 else 4)
            if off is None:
                off = self._off
                self._off += (bsz + 63) // 64 * 64
            assert off + bsz <= total, (name, off, bsz, total)
            self._sizes[name] = (off, bsz)
            return nc.alloc_sbuf_tensor_at(name, [128, cols], dtype, offset=self.base + off)

        self.wo = sb("wo", 8192, BF16)
        self.ident = sb("ident", 128, BF16)
        self.cmask = sb("cmask", 2048, BF16)
        self.ones = sb("ones", 128, BF16)
        self.consts = sb("consts", 8, F32)
        self.rT = sb("rT", 64, BF16)
        self.rTf = sb("rTf", 64, F32)
        self.rTlo = sb("rTlo", 64, BF16)
        self.masktm = sb("masktm", NTILE * 8, BF16)
        self.stats = sb("stats", 32, F32)
        self.otmpf = sb("otmpf", 512, F32)
        self.recipb = sb("recipb", 512, BF16, off=self._sizes["otmpf"][0])
        self.mrow = sb("mrow", 512, BF16)
        self.otmp2 = sb("otmp2", 512, BF16, off=self._sizes["mrow"][0])
        self.tmerge = sb("tmerge", 1024, F32)
        self.sgl = sb("sgl", 1024, BF16)

        r1 = self._off
        self.wq = sb("wq", 8192, BF16)
        self.wbuf0 = sb("wbuf0", 2048, BF16, off=r1)
        self.wbuf2 = sb("wbuf2", 2048, BF16, off=r1 + 4096)
        self.ofm = sb("ofm", 4096, BF16, off=r1 + 8192)
        self.x1lofm = sb("x1lofm", NCH * 512, BF16, off=r1 + 8192)
        r2 = self._off
        self.xfm = sb("xfm", NCH * SEQ, BF16)
        r3 = self._off
        self.qfm = sb("qfm", NCH * SEQ, BF16)
        self.zfm = sb("zfm", ICN * 512, BF16, off=r3)
        self.xefm = sb("xefm", NCH * 512, BF16, off=r3 + 22528)
        r4 = self._off
        self.kfm = sb("kfm", NCH * SEQ, BF16)
        self.yacc = sb("yacc", 4 * DM, F32, off=r4)
        self.vtm = sb("vtm", NTILE * NH * 65, BF16)
        self.xres = sb("xres", 4 * DM, F32)
        r7 = self._off
        self.pts = sb("pts", NTILE * 512, BF16)
        self.wbuf1 = sb("wbuf1", 2048, BF16, off=r7)
        self.wbuf3 = sb("wbuf3", 2048, BF16, off=r7 + 4096)
        self.dnbuf = [sb(f"dnbuf{p}", 1024, BF16, off=r7 + 8192 + p * 2048) for p in range(4)]
        self.xcast = sb("xcast", 4096, BF16, off=r7)
        r6 = self._sizes["xres"][0]
        self.stg_gu = [sb(f"stg_gu{p}", 4096, BF16, off=r6 + p * 8192) for p in range(2)]
        self.stg_dn = [sb(f"stg_dn{p}", 2048, BF16, off=r7 + 8192 + p * 4096) for p in range(2)]
        assert self._off <= total, self._off

        self.pf = E(nc.psum_tensor("pf", [128, 6 * 512], F32))
        self.pb = E(nc.psum_tensor("pb", [128, 2 * 1024], BF16))
        self.wbufs = [self.wbuf0, self.wbuf1, self.wbuf2, self.wbuf3]

        self.block_cm = nc.Block()
        self.block = self.block_cm.__enter__()

        self._init_phase()
        self._a1(0)
        self._precast()
        pc_tok = dict(self.cnt)
        for eng in "tvsq":
            for sname in ("wb0", "wb1", "sg0", "sg1"):
                self._wait(eng, (sname, pc_tok[sname]))
        for b in range(nseq):
            if b > 0:
                self._a1(b)
            self._attn(b)
            self._ffn(b)
            self.barrier()
        self._finish()

        blk = self.block
        st = self.stream
        blk.gpsimd(lambda e: [f(e) for f in st["g"]])
        blk.vector(lambda e: [f(e) for f in st["v"]])
        blk.scalar(lambda e: [f(e) for f in st["s"]])
        blk.tensor(lambda e: [f(e) for f in st["t"]])
        blk.sync(lambda e: [f(e) for f in st["q"]])

        self.block_cm.__exit__(None, None, None)
        for cm in reversed(self._ctx):
            cm.__exit__(None, None, None)
        self._cleanup.__exit__(None, None, None)
        lower_extended_insts(nc)

    def _wait(self, eng, tok):
        if tok is None:
            return
        s, v = tok
        if v <= 0 or self.hwm[eng].get(s, 0) >= v:
            return
        self.hwm[eng][s] = v
        sem = self.sem[s]
        self.stream[eng].append(lambda E, sem=sem, v=v: E.wait_ge(sem, v))

    ENG_SEM = {"t": "t", "v": "v", "s": "s", "q": "q", "g": "g"}

    def op(self, eng, fn, rd=(), wr=(), extra=(), inc=True, dsem=None):
        own = self.ENG_SEM[eng]
        deps = []
        for n in rd:
            t = self._wtok.get(n)
            if t is not None:
                deps.append(t)
        for n in wr:
            t = self._wtok.get(n)
            if t is not None and t[0] != own:
                deps.append(t)
            for oe, v in self._rtok.get(n, {}).items():
                deps.append((oe, v))
        deps.extend(x for x in extra if x is not None)
        for t in deps:
            if t[0] == own and eng == "t":
                continue
            self._wait(eng, t)
        if eng in "qg":
            s = dsem or own
            self.cnt[s] += 16
            val = self.cnt[s]
            sem = self.sem[s]
            self.stream[eng].append(
                lambda E, fn=fn, sem=sem: fn(E).then_inc(sem, 16))
            if dsem is None:
                self._wait(eng, (s, val))
            tok = (s, val)
        elif inc:
            self.cnt[own] += 1
            val = self.cnt[own]
            sem = self.sem[own]
            self.stream[eng].append(
                lambda E, fn=fn, sem=sem: fn(E).then_inc(sem, 1))
            tok = (own, val)
        else:
            self.stream[eng].append(lambda E, fn=fn: fn(E))
            tok = (own, self.cnt[own])
        for n in wr:
            self._wtok[n] = tok
            self._rtok[n] = {}
        for n in rd:
            self._rtok.setdefault(n, {})[tok[0]] = tok[1]
        return tok

    def barrier(self):
        snap = dict(self.cnt)
        for eng in "tvsqg":
            for s, v in snap.items():
                if s != self.ENG_SEM[eng]:
                    self._wait(eng, (s, v))

    def lp(self, fn):
        def g(E, fn=fn):
            with self.nc.allow_low_precision(reason="bf16 operand for following matmul"):
                return fn(E)
        return g

    def bank(self, i):
        return self.pf[:, i * 512:(i + 1) * 512]

    def tbank(self, i):
        return self.pb[:, i * 1024:(i + 1) * 1024]

    def _init_phase(self):
        v = "v"
        self.op(v, lambda E: E.memset(self.consts[:, 0:1], -30.0), wr=("consts",))
        self.op(v, lambda E: E.memset(self.consts[:, 1:2], 1e-5), wr=("consts",))
        self.op(v, lambda E: E.memset(self.ones[:], 1.0), wr=("ones",))
        vr = self.vtm[:].rearrange("p (t h c) -> p t h c", h=NH, c=65)
        self.op(v, lambda E: E.memset(vr[:, :, :, 64:65], 1.0), wr=("vtm",))
        self.op("g", lambda E: E.dma_start(self.ident[:], self.identd), wr=("ident",))
        self.op("g", lambda E: E.dma_start(self.cmask[:], self.cmaskd), wr=("cmask",))
        self.op("g", lambda E: E.dma_start(
            self.rT[:].rearrange("p (c e) -> p c e", e=8),
            self.rTd.rearrange("(c p) e -> p c e", p=128)), wr=("rT",))
        self.op("g", lambda E: E.dma_start(
            self.rTf[:].rearrange("p (c e) -> p c e", e=8),
            self.rTd.rearrange("(c p) e -> p c e", p=128)), wr=("rTf",))
        self.op("v", self.lp(lambda E: E.tensor_tensor(
            self.rTlo[:], self.rTf[:], self.rT[:], ALU.subtract)),
            rd=("rTf", "rT"), wr=("rTlo",))
        self.op("g", lambda E: E.dma_start(
            self.wo[:].rearrange("p (j n) -> p j n", j=8), self.woTp), wr=("wo",))

    def _precast(self):
        for e in range(NE):
            gT_e = self.gTd[e].rearrange("(c p) i -> p c i", p=128)
            uT_e = self.uTd[e].rearrange("(c p) i -> p c i", p=128)
            for grp in range(WG):
                par = grp % 2
                sg, sd = self.stg_gu[par], self.stg_dn[par]
                ng, nd = f"stg_gu{par}", f"stg_dn{par}"
                for sub in range(2):
                    ic = grp * 2 + sub
                    i0 = ic * 128
                    self.op("g", lambda E, sg=sg, s=sub, gT_e=gT_e, i0=i0: E.dma_start(
                        sg[:, s * 2048:s * 2048 + 1024].rearrange("p (c i) -> p c i", c=8),
                        gT_e[:, :, i0:i0 + 128]), wr=(ng,), dsem=f"sg{par}")
                    self.op("g", lambda E, sg=sg, s=sub, uT_e=uT_e, i0=i0: E.dma_start(
                        sg[:, s * 2048 + 1024:s * 2048 + 2048].rearrange("p (c i) -> p c i", c=8),
                        uT_e[:, :, i0:i0 + 128]), wr=(ng,), dsem=f"sg{par}")
                    self.op("g", lambda E, sd=sd, s=sub, e=e, i0=i0: E.dma_start(
                        sd[:, s * 1024:s * 1024 + 1024],
                        self.dTd[e, i0:i0 + 128, :]), wr=(nd,), dsem=f"sg{par}")
                self.op("q", lambda E, e=e, grp=grp, sg=sg: E.dma_start(
                    self.wgu[e, grp], sg[:]), rd=(ng,), wr=(f"wgu{e}_{grp}",),
                    dsem=f"wb{par}")
                self.op("q", lambda E, e=e, grp=grp, sd=sd: E.dma_start(
                    self.wdn[e, grp], sd[:]), rd=(nd,), wr=(f"wdn{e}_{grp}",),
                    dsem=f"wb{par}")

    def _a1(self, b):
        nc = self.nc
        XCN = tuple(f"pts{t}" for t in range(8)) + ("xcast",)
        for g in range(NSTRIP):
            src = self.xin[b, g * 512:(g + 1) * 512, :].rearrange("(t p) d -> p t d", p=128)
            self.op("g", lambda E, src=src: E.dma_start(
                self.xcast[:].rearrange("p (t d) -> p t d", t=4), src), wr=XCN)
            for c in range(NCH):
                tb = (g * NCH + c) % 2
                tname = f"T{tb}"
                for t in range(4):
                    self.op("t", lambda E, tb=tb, t=t, c=c: E.transpose(
                        self.tbank(tb)[:, t * 128:(t + 1) * 128],
                        self.xcast[:, t * 1024 + c * 128:t * 1024 + c * 128 + 128],
                        self.ident[:]),
                        rd=XCN + ("ident",), wr=(tname,), inc=(t == 3))
                self.op("v", lambda E, tb=tb, c=c, g=g: E.tensor_copy(
                    self.xfm[:, c * SEQ + g * 512:c * SEQ + g * 512 + 512],
                    self.tbank(tb)[:, 0:512]), rd=(tname,), wr=("xfm",))
        for wname, dst, dstn, wsrc in (("wq_q", self.qfm, "qfm", self.wqT),
                                       ("wq_k", self.kfm, "kfm", self.wkT)):
            self.op("g", lambda E, wsrc=wsrc: E.dma_start(
                self.wq[:].rearrange("p (c n) -> p c n", c=8),
                wsrc.rearrange("(c p) n -> p c n", p=128)), wr=("wq",))
            for c in range(NCH):
                for d in range(NCH):
                    for g in range(NSTRIP):
                        self.op("t", lambda E, g=g, d=d, c=c: E.matmul(
                            self.bank(g),
                            self.wq[:, d * 1024 + c * 128:d * 1024 + c * 128 + 128],
                            self.xfm[:, d * SEQ + g * 512:d * SEQ + g * 512 + 512],
                            start=(d == 0), stop=(d == NCH - 1), skip_group_check=True),
                            rd=("wq", "xfm"), wr=(f"B{g}",),
                            inc=(d == NCH - 1))
                for g in range(NSTRIP):
                    self.op("v", lambda E, g=g, c=c, dst=dst: E.tensor_copy(
                        dst[:, c * SEQ + g * 512:c * SEQ + g * 512 + 512],
                        self.bank(g)), rd=(f"B{g}",), wr=(dstn,))
        self.op("g", lambda E: E.dma_start(
            self.wq[:].rearrange("p (c n) -> p c n", c=8),
            self.wvT.rearrange("(c p) n -> p c n", p=128)), wr=("wq",))
        vr = self.vtm[:].rearrange("p (t h c) -> p t h c", h=NH, c=65)
        for t in range(NTILE):
            pair = (4, 5) if t % 2 == 0 else (2, 3)
            for d in range(NCH):
                for hlf in range(2):
                    self.op("t", lambda E, pair=pair, hlf=hlf, d=d, t=t: E.matmul(
                        self.bank(pair[hlf]),
                        self.xfm[:, d * SEQ + t * 128:d * SEQ + t * 128 + 128],
                        self.wq[:, d * 1024 + hlf * 512:d * 1024 + hlf * 512 + 512],
                        start=(d == 0), stop=(d == NCH - 1), skip_group_check=True),
                        rd=("wq", "xfm"), wr=(f"B{pair[hlf]}",),
                        inc=(d == NCH - 1))
            for hlf in range(2):
                self.op("v", lambda E, pair=pair, hlf=hlf, t=t, vr=vr: E.tensor_copy(
                    vr[:, t, hlf * 8:hlf * 8 + 8, 0:64],
                    self.bank(pair[hlf]).rearrange("p (h c) -> p h c", c=64)),
                    rd=(f"B{pair[hlf]}",), wr=("vtm",))
        if "qkv" in self.taps and b == 0:
            self.op("q", lambda E: E.dma_start(self.tapd["tq"], self.qfm[:]), rd=("qfm",))
            self.op("q", lambda E: E.dma_start(self.tapd["tk"], self.kfm[:]), rd=("kfm",))
            tok = self.op("q", lambda E: E.dma_start(self.tapd["tv"], self.vtm[:]), rd=("vtm",))
            self.out_toks.append(tok)

    def _attn(self, b):
        cneg = self.consts[:, 0:1]
        ceps = self.consts[:, 1:2]
        for g in range(NSTRIP):
            src = self.xin[b, g * 512:(g + 1) * 512, :].rearrange("(t p) d -> p t d", p=128)
            self.op("q", lambda E, src=src: E.dma_start(
                self.xres[:].rearrange("p (t d) -> p t d", t=4), src), wr=("xres",))
            T = 4 * g + 4

            def opr_tile(h, t):
                self.op("t", lambda E, t=t, h=h: E.matmul(
                    self.bank(2)[0:65, :],
                    self.vtm[:, t * 1040 + h * 65:t * 1040 + h * 65 + 65],
                    self.pts[:, t * 512:(t + 1) * 512],
                    start=(t == 0), stop=(t == T - 1), skip_group_check=True),
                    rd=("vtm", f"pts{t}"), wr=("B2",))

            def head_scores(h):
                hr = (h % 2) * 64
                ch = h // 2
                for t0 in range(0, T, 2):
                    qd = (t0 // 2) % 2
                    bks = (0, 1) if qd == 0 else (4, 5)
                    for j, t in enumerate((t0, t0 + 1)):
                        self.op("t", lambda E, bk=bks[j], hr=hr, ch=ch, t=t, g=g: E.matmul(
                            self.bank(bk),
                            self.kfm[hr:hr + 64, ch * SEQ + t * 128:ch * SEQ + t * 128 + 128],
                            self.qfm[hr:hr + 64, ch * SEQ + g * 512:ch * SEQ + g * 512 + 512],
                            start=True, stop=True, tile_position=(hr, 0),
                            skip_group_check=True),
                            rd=("kfm", "qfm"), wr=(f"B{bks[j]}",))
                    off = 0 if qd == 0 else 2048
                    self.op("s", lambda E, t0=t0, off=off: E.activation(
                        self.pts[:, t0 * 512:(t0 + 2) * 512],
                        self.pf[:, off:off + 1024],
                        AF.Exp, bias=cneg, scale=0.125),
                        rd=(f"B{bks[0]}", f"B{bks[1]}", "consts"),
                        wr=(f"pts{t0}", f"pts{t0 + 1}"))
                    for t in (t0, t0 + 1):
                        r = t - 4 * g
                        if r >= 0:
                            self.op("v", lambda E, t=t, r=r: E.tensor_tensor(
                                self.pts[:, t * 512:(t + 1) * 512],
                                self.pts[:, t * 512:(t + 1) * 512],
                                self.cmask[:, r * 512:(r + 1) * 512], ALU.mult),
                                rd=(f"pts{t}", "cmask"), wr=(f"pts{t}",))

            def head_opr(h):
                for t in range(T):
                    opr_tile(h, t)
                self.op("v", lambda E: E.tensor_copy(
                    self.otmpf[0:64, :], self.bank(2)[0:64, :]),
                    rd=("B2",), wr=("otmpf",))
                self.op("v", self.lp(lambda E: E.reciprocal(
                    self.recipb[64:65, 0:512], self.bank(2)[64:65, :])),
                    rd=("B2",), wr=("recipb",))

            def head_bcast(h):
                self.op("t", lambda E: E.matmul(
                    self.bank(3), self.ones[64:65, 0:128], self.recipb[64:65, 0:512],
                    start=True, stop=True, tile_position=(64, 0), skip_group_check=True),
                    rd=("ones", "recipb"), wr=("B3",))

            def head_mul(h):
                if h < 8:
                    self.op("v", lambda E, h=h: E.tensor_tensor(
                        self.ofm[0:64, h * 512:(h + 1) * 512],
                        self.otmpf[0:64, :], self.bank(3)[0:64, :], ALU.mult),
                        rd=("otmpf", "B3"), wr=("ofm",))
                else:
                    self.op("v", lambda E: E.tensor_tensor(
                        self.otmp2[0:64, :], self.otmpf[0:64, :],
                        self.bank(3)[0:64, :], ALU.mult),
                        rd=("otmpf", "B3"), wr=("otmp2",))

            def head_shift(h):
                if h >= 8:
                    self.op("t", lambda E: E.matmul(
                        self.bank(3)[64:128, :], self.ident[0:64, 0:64],
                        self.otmp2[0:64, 0:512], start=True, stop=True,
                        tile_position=(0, 64), skip_group_check=True),
                        rd=("ident", "otmp2"), wr=("B3",))
                    self.op("v", lambda E, h=h: E.tensor_copy(
                        self.ofm[64:128, (h - 8) * 512:(h - 7) * 512],
                        self.bank(3)[64:128, :]), rd=("B3",), wr=("ofm",))

            for h in range(NH):
                head_scores(h)
                if h >= 1:
                    head_bcast(h - 1)
                    head_mul(h - 1)
                head_opr(h)
                if h >= 1:
                    head_shift(h - 1)
            head_bcast(NH - 1)
            head_mul(NH - 1)
            head_shift(NH - 1)
            if "ofm" in self.taps and b == 0 and g == 0:
                tok = self.op("q", lambda E: E.dma_start(self.tapd["tofm"], self.ofm[:]), rd=("ofm",))
                self.out_toks.append(tok)
            for qt in range(4):
                for j in range(NCH):
                    for hlf in range(2):
                        bk0, bk1 = (2, 3) if hlf == 0 else (4, 5)
                        self.op("t", lambda E, j=j, hlf=hlf, qt=qt, bk0=bk0: E.matmul(
                            self.bank(bk0),
                            self.ofm[0:64, j * 512 + qt * 128:j * 512 + qt * 128 + 128],
                            self.wo[0:64, j * 1024 + hlf * 512:j * 1024 + hlf * 512 + 512],
                            start=(j == 0), stop=(j == NCH - 1),
                            tile_position=(0, 0), skip_group_check=True),
                            rd=("ofm", "wo"), wr=(f"B{bk0}",), inc=(j == NCH - 1))
                        self.op("t", lambda E, j=j, hlf=hlf, qt=qt, bk1=bk1: E.matmul(
                            self.bank(bk1),
                            self.ofm[64:128, j * 512 + qt * 128:j * 512 + qt * 128 + 128],
                            self.wo[64:128, j * 1024 + hlf * 512:j * 1024 + hlf * 512 + 512],
                            start=(j == 0), stop=(j == NCH - 1),
                            tile_position=(64, 0), skip_group_check=True),
                            rd=("ofm", "wo"), wr=(f"B{bk1}",), inc=(j == NCH - 1))
                xt = self.xres[:, qt * 1024:(qt + 1) * 1024]
                self.op("v", lambda E, xt=xt: E.tensor_tensor(
                    self.tmerge[:, 0:512], xt[:, 0:512], self.bank(2), ALU.add),
                    rd=("B2", "xres"), wr=("tmerge",))
                self.op("v", lambda E: E.tensor_tensor(
                    self.tmerge[:, 0:512], self.tmerge[:, 0:512], self.bank(3), ALU.add),
                    rd=("B3", "tmerge"), wr=("tmerge",))
                self.op("v", lambda E, xt=xt: E.tensor_tensor(
                    self.tmerge[:, 512:1024], xt[:, 512:1024], self.bank(4), ALU.add),
                    rd=("B4", "xres"), wr=("tmerge",))
                self.op("v", lambda E: E.tensor_tensor(
                    self.tmerge[:, 512:1024], self.tmerge[:, 512:1024], self.bank(5), ALU.add),
                    rd=("B5", "tmerge"), wr=("tmerge",))
                self.op("v", lambda E, xt=xt, qt=qt: E.tensor_tensor_reduce(
                    out=xt, in0=self.tmerge[:], in1=self.tmerge[:], scale=0.5,
                    scalar=0.0, op0=ALU.add, op1=ALU.add,
                    accum_out=self.stats[:, qt:qt + 1]),
                    rd=("tmerge",), wr=("xres", "stats"))
                self.op("s", lambda E, xt=xt, qt=qt: E.activation(
                    self.tmerge[:], xt, AF.Square,
                    accum_out=self.stats[:, 4 + qt:5 + qt]),
                    rd=("xres",), wr=("tmerge", "stats"))
                self._ln_qt(self.stats, ceps, qt)
                xfm_r = self.xfm[:].rearrange("p (c s) -> p c s", s=SEQ)
                tn = (f"pts{2 * qt}", f"pts{2 * qt + 1}", "xcast")
                self.op("s", lambda E, xt=xt, qt=qt: E.activation(
                    xt, xt, AF.Identity, bias=self.stats[:, 12 + qt:13 + qt],
                    scale=self.stats[:, 24 + qt:25 + qt]),
                    rd=("stats", "xres"), wr=("xres",))
                self.op("v", lambda E, xt=xt, qt=qt: E.tensor_copy(
                    self.xcast[:, qt * 1024:(qt + 1) * 1024], xt),
                    rd=("xres",), wr=tn)
                for c in range(NCH):
                    tb = c // 4
                    self.op("t", lambda E, tb=tb, qt=qt, c=c: E.transpose(
                        self.tbank(tb)[:, (c % 4) * 128:(c % 4) * 128 + 128],
                        self.xcast[:, qt * 1024 + c * 128:qt * 1024 + c * 128 + 128],
                        self.ident[:]),
                        rd=tn + ("ident",), wr=(f"T{tb}",), inc=(c % 4 == 3))
                for tb in range(2):
                    self.op("v", lambda E, tb=tb, qt=qt, g=g, xfm_r=xfm_r: E.tensor_copy(
                        xfm_r[:, tb * 4:tb * 4 + 4, g * 512 + qt * 128:g * 512 + qt * 128 + 128],
                        self.tbank(tb)[:, 0:512].rearrange("p (c k) -> p c k", k=128)),
                        rd=(f"T{tb}",), wr=("xfm",))
            dst = self.x1scr[g * 512:(g + 1) * 512, :].rearrange("(t p) d -> p t d", p=128)
            tok = self.op("q", lambda E, dst=dst: E.dma_start(
                dst, self.xres[:].rearrange("p (t d) -> p t d", t=4)),
                rd=("xres",), wr=(f"x1s{g}",))
            if "x1" in self.taps and b == 0:
                dst2 = self.tapd["tx1"][g * 512:(g + 1) * 512, :].rearrange("(t p) d -> p t d", p=128)
                tok = self.op("q", lambda E, dst2=dst2: E.dma_start(
                    dst2, self.xres[:].rearrange("p (t d) -> p t d", t=4)), rd=("xres",))
                self.out_toks.append(tok)
            XCN = tuple(f"pts{t}" for t in range(8)) + ("xcast",)
            self.op("v", self.lp(lambda E: E.tensor_tensor(
                self.xcast[:], self.xres[:], self.xcast[:], ALU.subtract)),
                rd=("xres",) + XCN, wr=XCN)
            for c in range(NCH):
                tb = c % 2
                for t in range(4):
                    self.op("t", lambda E, tb=tb, t=t, c=c: E.transpose(
                        self.tbank(tb)[:, t * 128:(t + 1) * 128],
                        self.xcast[:, t * 1024 + c * 128:t * 1024 + c * 128 + 128],
                        self.ident[:]),
                        rd=XCN + ("ident",), wr=(f"T{tb}",), inc=(t == 3))
                self.op("v", lambda E, tb=tb, c=c: E.tensor_copy(
                    self.x1lofm[:, c * 512:(c + 1) * 512],
                    self.tbank(tb)[:, 0:512]), rd=(f"T{tb}",), wr=("x1lofm", "ofm"))
            if "lof" in self.taps and b == 0 and g == 0:
                tok = self.op("q", lambda E: E.dma_start(self.tapd["tlof"], self.x1lofm[:]),
                              rd=("x1lofm", "ofm"))
                self.out_toks.append(tok)
            for qt in range(4):
                Q = g * 4 + qt
                nmm = 3 * NCH
                i = 0
                for lhs_sel, rhs_sel in (("hi", "rT"), ("hi", "rTlo"), ("lo", "rT")):
                    for c in range(NCH):
                        if lhs_sel == "hi":
                            lap = self.xfm[:, c * SEQ + Q * 128:c * SEQ + Q * 128 + 128]
                            lrd = "xfm"
                        else:
                            lap = self.x1lofm[:, c * 512 + qt * 128:c * 512 + qt * 128 + 128]
                            lrd = "ofm"
                        rap = (self.rT if rhs_sel == "rT" else self.rTlo)[:, c * 8:c * 8 + 8]
                        i += 1
                        self.op("t", lambda E, qt=qt, lap=lap, rap=rap, i=i, nmm=nmm: E.matmul(
                            self.bank(0)[:, qt * 8:qt * 8 + 8], lap, rap,
                            start=(i == 1), stop=(i == nmm), skip_group_check=True),
                            rd=(lrd, rhs_sel), wr=("B0",), inc=(i == nmm))
                self.op("v", lambda E, qt=qt: E.reduce_max(
                    self.stats[:, 28 + qt:29 + qt], self.bank(0)[:, qt * 8:qt * 8 + 8],
                    axis=AX.X), rd=("B0",), wr=("stats",))
                self.op("v", lambda E, qt=qt, Q=Q: E.tensor_scalar(
                    self.masktm[:, Q * 8:Q * 8 + 8], self.bank(0)[:, qt * 8:qt * 8 + 8],
                    self.stats[:, 28 + qt:29 + qt], None, ALU.is_ge),
                    rd=("B0", "stats"), wr=("masktm",))
        if "mask" in self.taps and b == 0:
            tok = self.op("q", lambda E: E.dma_start(self.tapd["tmask"], self.masktm[:]),
                          rd=("masktm",))
            self.out_toks.append(tok)

    def _ln_qt(self, stats, ceps, qt):
        a, b2, c3, d, e2, f = (qt, 4 + qt, 8 + qt, 12 + qt, 16 + qt, 20 + qt)
        r = 24 + qt
        self.op("v", lambda E: E.tensor_scalar_mul(
            stats[:, c3:c3 + 1], stats[:, a:a + 1], -1.0 / DM), rd=("stats",), wr=("stats",))
        self.op("v", lambda E: E.tensor_tensor(
            stats[:, d:d + 1], stats[:, c3:c3 + 1], stats[:, c3:c3 + 1], ALU.mult),
            rd=("stats",), wr=("stats",))
        self.op("v", lambda E: E.scalar_tensor_tensor(
            stats[:, e2:e2 + 1], stats[:, b2:b2 + 1], 1.0 / DM, stats[:, d:d + 1],
            ALU.mult, ALU.subtract), rd=("stats",), wr=("stats",))
        self.op("s", lambda E: E.activation(
            stats[:, f:f + 1], stats[:, e2:e2 + 1], AF.Sqrt, bias=ceps),
            rd=("stats", "consts"), wr=("stats",))
        self.op("v", lambda E: E.reciprocal(
            stats[:, r:r + 1], stats[:, f:f + 1]), rd=("stats",), wr=("stats",))
        self.op("v", lambda E: E.tensor_tensor(
            stats[:, d:d + 1], stats[:, c3:c3 + 1], stats[:, r:r + 1], ALU.mult),
            rd=("stats",), wr=("stats",))

    def _ln(self, stats, ceps):
        self.op("v", lambda E: E.tensor_scalar_mul(
            stats[:, 8:12], stats[:, 0:4], -1.0 / DM), rd=("stats",), wr=("stats",))
        self.op("v", lambda E: E.tensor_tensor(
            stats[:, 12:16], stats[:, 8:12], stats[:, 8:12], ALU.mult),
            rd=("stats",), wr=("stats",))
        self.op("v", lambda E: E.scalar_tensor_tensor(
            stats[:, 16:20], stats[:, 4:8], 1.0 / DM, stats[:, 12:16],
            ALU.mult, ALU.subtract), rd=("stats",), wr=("stats",))
        self.op("s", lambda E: E.activation(
            stats[:, 20:24], stats[:, 16:20], AF.Sqrt, bias=ceps),
            rd=("stats", "consts"), wr=("stats",))
        self.op("v", lambda E: E.reciprocal(
            stats[:, 24:28], stats[:, 20:24]), rd=("stats",), wr=("stats",))
        self.op("v", lambda E: E.tensor_tensor(
            stats[:, 12:16], stats[:, 8:12], stats[:, 24:28], ALU.mult),
            rd=("stats",), wr=("stats",))

    def _ffn(self, b):
        ceps = self.consts[:, 1:2]
        self._wait("q", ("t", self.cnt["t"]))
        self._wait("q", ("v", self.cnt["v"]))
        def reload(g):
            src = self.x1scr[g * 512:(g + 1) * 512, :].rearrange("(t p) d -> p t d", p=128)
            self.op("q", lambda E, src=src: E.dma_start(
                self.xres[:].rearrange("p (t d) -> p t d", t=4), src),
                rd=(f"x1s{g}",), wr=("xres",))

        reload(0)
        pending = [None]
        for g in range(NSTRIP):
            def prep(e, g=g):
                for qt in range(4):
                    Q = g * 4 + qt
                    self.op("t", lambda E, qt=qt, Q=Q, e=e: E.transpose(
                        self.tbank(0)[0:1, qt * 128:qt * 128 + 128],
                        self.masktm[:, Q * 8 + e:Q * 8 + e + 1], self.ident[:]),
                        rd=("masktm", "ident"), wr=("T0",), inc=(qt == 3))
                self.op("v", lambda E: E.tensor_copy(
                    self.mrow[0:1, 0:512], self.tbank(0)[0:1, 0:512]),
                    rd=("T0",), wr=("mrow",))
                self.op("t", lambda E: E.matmul(
                    self.bank(0), self.ones[0:1, 0:128], self.mrow[0:1, 0:512],
                    start=True, stop=True, skip_group_check=True),
                    rd=("ones", "mrow"), wr=("B0",))
                for c in range(NCH):
                    self.op("v", lambda E, c=c, g=g: E.tensor_tensor(
                        self.xefm[:, c * 512:(c + 1) * 512],
                        self.xfm[:, c * SEQ + g * 512:c * SEQ + g * 512 + 512],
                        self.bank(0), ALU.mult), rd=("xfm", "B0"), wr=("xefm",))

            if g == 0:
                prep(0)
            for e in range(NE):
                if e == 1 and pending[0] is not None:
                    pending[0]()
                    pending[0] = None
                for ic in range(ICN):
                    par = ic % 4
                    wb = self.wbufs[par]
                    wn = f"wbuf{par}"
                    wsl = self.wgu[e, ic // 2][:, (ic % 2) * 2048:(ic % 2) * 2048 + 2048]
                    self.op("q" if par % 2 == 0 else "g",
                        lambda E, wb=wb, wsl=wsl: E.dma_start(wb[:], wsl),
                        rd=(f"wgu{e}_{ic // 2}",), wr=(wn,), dsem=f"w{par}")
                    gbk, ubk = (0, 1) if ic % 2 == 0 else (2, 3)
                    for d in range(NCH):
                        self.op("t", lambda E, wb=wb, d=d, gbk=gbk: E.matmul(
                            self.bank(gbk),
                            wb[:, d * 128:d * 128 + 128],
                            self.xefm[:, d * 512:(d + 1) * 512],
                            start=(d == 0), stop=(d == NCH - 1), skip_group_check=True),
                            rd=(wn, "xefm"), wr=(f"B{gbk}",), inc=(d == NCH - 1))
                    for d in range(NCH):
                        self.op("t", lambda E, wb=wb, d=d, ubk=ubk: E.matmul(
                            self.bank(ubk),
                            wb[:, 1024 + d * 128:1024 + d * 128 + 128],
                            self.xefm[:, d * 512:(d + 1) * 512],
                            start=(d == 0), stop=(d == NCH - 1), skip_group_check=True),
                            rd=(wn, "xefm"), wr=(f"B{ubk}",), inc=(d == NCH - 1))
                    sp = ic % 2
                    self.op("s", lambda E, gbk=gbk, sp=sp: E.activation(
                        self.sgl[:, sp * 512:(sp + 1) * 512], self.bank(gbk), AF.Silu),
                        rd=(f"B{gbk}",), wr=(f"sgl{sp}",))
                    self.op("v", lambda E, ic=ic, sp=sp, ubk=ubk: E.tensor_tensor(
                        self.zfm[:, ic * 512:(ic + 1) * 512],
                        self.sgl[:, sp * 512:(sp + 1) * 512],
                        self.bank(ubk), ALU.mult),
                        rd=(f"sgl{sp}", f"B{ubk}"), wr=("zfm",))
                if e + 1 < NE:
                    prep(e + 1)
                elif g + 1 < NSTRIP:
                    prep(0, g + 1)
                for hlf in range(2):
                    for grp in range(WG):
                        par = (hlf * WG + grp) % 4
                        db = self.dnbuf[par]
                        dn = f"dnbuf{par}"
                        wsl = self.wdn[e, grp].rearrange(
                            "p (s c) -> p s c", s=2)[:, :, hlf * 512:hlf * 512 + 512]
                        self.op("q" if par == 0 else "g",
                            lambda E, db=db, wsl=wsl: E.dma_start(
                                db[:, 0:1024].rearrange("p (s c) -> p s c", s=2), wsl),
                            rd=(f"wdn{e}_{grp}",), wr=(dn,), dsem=f"d{par}")
                        for sub in range(2):
                            ic = grp * 2 + sub
                            for tt in range(4):
                                bk = 2 + tt
                                self.op("t", lambda E, ic=ic, tt=tt, db=db,
                                        sub=sub, bk=bk: E.matmul(
                                    self.bank(bk),
                                    self.zfm[:, ic * 512 + tt * 128:ic * 512 + tt * 128 + 128],
                                    db[:, sub * 512:sub * 512 + 512],
                                    start=(ic == 0), stop=(ic == ICN - 1),
                                    skip_group_check=True),
                                    rd=(dn, "zfm"), wr=(f"B{bk}",))
                    for tt in range(4):
                        bk = 2 + tt
                        dstc = tt * 1024 + hlf * 512
                        if e == 0:
                            self.op("v", lambda E, bk=bk, dstc=dstc: E.tensor_copy(
                                self.yacc[:, dstc:dstc + 512], self.bank(bk)),
                                rd=(f"B{bk}",), wr=("yacc",))
                        else:
                            self.op("v", lambda E, bk=bk, dstc=dstc: E.tensor_tensor(
                                self.yacc[:, dstc:dstc + 512],
                                self.yacc[:, dstc:dstc + 512],
                                self.bank(bk), ALU.add),
                                rd=(f"B{bk}", "yacc"), wr=("yacc",))
            for qt in range(4):
                xt = self.xres[:, qt * 1024:(qt + 1) * 1024]
                self.op("v", lambda E, xt=xt, qt=qt: E.tensor_tensor_reduce(
                    out=xt, in0=xt, in1=self.yacc[:, qt * 1024:(qt + 1) * 1024],
                    scale=1.0, scalar=0.0, op0=ALU.add, op1=ALU.add,
                    accum_out=self.stats[:, qt:qt + 1]),
                    rd=("yacc", "xres"), wr=("xres", "stats"))
                self.op("s", lambda E, xt=xt, qt=qt: E.activation(
                    self.tmerge[:], xt, AF.Square,
                    accum_out=self.stats[:, 4 + qt:5 + qt]),
                    rd=("xres",), wr=("tmerge", "stats"))
            self._ln(self.stats, ceps)
            for qt in range(4):
                xt = self.xres[:, qt * 1024:(qt + 1) * 1024]
                self.op("s", lambda E, xt=xt, qt=qt: E.activation(
                    xt, xt, AF.Identity, bias=self.stats[:, 12 + qt:13 + qt],
                    scale=self.stats[:, 24 + qt:25 + qt]),
                    rd=("stats", "xres"), wr=("xres",))
            def emit_out(g=g):
                dst = self.outd[b, g * 512:(g + 1) * 512, :].rearrange("(t p) d -> p t d", p=128)
                tok = self.op("q", lambda E, dst=dst: E.dma_start(
                    dst, self.xres[:].rearrange("p (t d) -> p t d", t=4)), rd=("xres",))
                self.out_toks.append(tok)
                if g + 1 < NSTRIP:
                    reload(g + 1)
            pending[0] = emit_out
        if pending[0] is not None:
            pending[0]()
            pending[0] = None

    def _finish(self):
        for tok in self.out_toks:
            self._wait("q", tok)
        self.barrier()


def _host_prep(L0_attn_X, L0_attn_q_proj, L0_attn_k_proj, L0_attn_v_proj,
               L0_attn_o_proj, L0_ffn_router_w, L0_ffn_gate_w, L0_ffn_up_w,
               L0_ffn_down_w):
    X = np.ascontiguousarray(np.asarray(L0_attn_X, dtype=np.float32))
    wqT = np.ascontiguousarray(np.asarray(L0_attn_q_proj, dtype=np.float32).T)
    wkT = np.ascontiguousarray(np.asarray(L0_attn_k_proj, dtype=np.float32).T)
    wvT = np.ascontiguousarray(np.asarray(L0_attn_v_proj, dtype=np.float32).T)
    woT = np.asarray(L0_attn_o_proj, dtype=np.float32).T
    w16 = woT.reshape(16, 64, DM)
    woTp = np.ascontiguousarray(np.concatenate(
        [w16[0:8].transpose(1, 0, 2), w16[8:16].transpose(1, 0, 2)], axis=0))
    rT = np.ascontiguousarray(np.asarray(L0_ffn_router_w, dtype=np.float32).T)
    gT = np.ascontiguousarray(np.asarray(L0_ffn_gate_w, dtype=np.float32).transpose(0, 2, 1))
    uT = np.ascontiguousarray(np.asarray(L0_ffn_up_w, dtype=np.float32).transpose(0, 2, 1))
    dT = np.ascontiguousarray(np.asarray(L0_ffn_down_w, dtype=np.float32).transpose(0, 2, 1))
    ident = np.eye(128, dtype=BFNP)
    cm = np.zeros((128, 4 * 512), dtype=BFNP)
    qq = np.arange(512)[None, :]
    ss = np.arange(128)[:, None]
    for r in range(4):
        cm[:, r * 512:(r + 1) * 512] = (qq >= r * 128 + ss).astype(BFNP)
    shared = {"wqT": wqT, "wkT": wkT, "wvT": wvT, "woTp": woTp, "rTd": rT,
              "gTd": gT, "uTd": uT, "dTd": dT, "identd": ident, "cmaskd": cm}
    return X, shared


def kernel(**inputs):
    X, shared = _host_prep(**inputs)
    mk = MK(nseq=4)
    in_maps = []
    for c in range(NCORES):
        m = dict(shared)
        m["xin"] = np.ascontiguousarray(X[c * 4:(c + 1) * 4])
        in_maps.append(m)
    res = run_bass_kernel_spmd(mk.nc, in_maps, core_ids=list(range(NCORES)))
    return np.concatenate([res.results[c]["outd"] for c in range(NCORES)], axis=0)


# version 14
# speedup vs baseline: 1.0215x; 1.0126x over previous
import numpy as np
import ml_dtypes
import concourse.bass as bass
import concourse.mybir as mybir
from concourse.bass_utils import run_bass_kernel_spmd
from concourse.library_overlay import lower_extended_insts

F32 = mybir.dt.float32
BF16 = mybir.dt.bfloat16
AF = mybir.ActivationFunctionType
ALU = mybir.AluOpType
AX = mybir.AxisListType
BFNP = ml_dtypes.bfloat16

NCORES = 8
SEQ, DM, NH, DK = 2048, 1024, 16, 64
NSTRIP, STRIP = 4, 512
NTILE = SEQ // 128
NCH = DM // 128
FI, ICN = 2816, 22
NE = 8
WG = 11

KB = 1024


class MK:
    def __init__(self, nseq=4, taps=()):
        self.nseq = nseq
        self.taps = set(taps)
        nc = bass.Bass("TRN2", target_bir_lowering=False)
        self.nc = nc
        self.xin = nc.dram_tensor("xin", [nseq, SEQ, DM], F32, kind="ExternalInput").ap()
        self.wqT = nc.dram_tensor("wqT", [DM, DM], F32, kind="ExternalInput").ap()
        self.wkT = nc.dram_tensor("wkT", [DM, DM], F32, kind="ExternalInput").ap()
        self.wvT = nc.dram_tensor("wvT", [DM, DM], F32, kind="ExternalInput").ap()
        self.woTp = nc.dram_tensor("woTp", [128, 8, DM], F32, kind="ExternalInput").ap()
        self.rTd = nc.dram_tensor("rTd", [DM, 8], F32, kind="ExternalInput").ap()
        self.gTd = nc.dram_tensor("gTd", [NE, DM, FI], F32, kind="ExternalInput").ap()
        self.uTd = nc.dram_tensor("uTd", [NE, DM, FI], F32, kind="ExternalInput").ap()
        self.dTd = nc.dram_tensor("dTd", [NE, FI, DM], F32, kind="ExternalInput").ap()
        self.identd = nc.dram_tensor("identd", [128, 128], BF16, kind="ExternalInput").ap()
        self.cmaskd = nc.dram_tensor("cmaskd", [128, 4 * 512], BF16, kind="ExternalInput").ap()
        self.outd = nc.dram_tensor("outd", [nseq, SEQ, DM], F32, kind="ExternalOutput").ap()
        self.x1scr = nc.dram_tensor("x1scr", [SEQ, DM], F32).ap()
        self.wgu = nc.dram_tensor("wgu", [NE, WG, 128, 4096], BF16).ap()
        self.wdn = nc.dram_tensor("wdn", [NE, WG, 128, 2048], BF16).ap()
        self.tapd = {}
        if "qkv" in self.taps:
            self.tapd["tq"] = nc.dram_tensor("tq", [128, NCH * SEQ], BF16, kind="ExternalOutput").ap()
            self.tapd["tk"] = nc.dram_tensor("tk", [128, NCH * SEQ], BF16, kind="ExternalOutput").ap()
            self.tapd["tv"] = nc.dram_tensor("tv", [128, NTILE * NH * 66], BF16, kind="ExternalOutput").ap()
        if "x1" in self.taps:
            self.tapd["tx1"] = nc.dram_tensor("tx1", [SEQ, DM], F32, kind="ExternalOutput").ap()
        if "mask" in self.taps:
            self.tapd["tmask"] = nc.dram_tensor("tmask", [128, NTILE * 8], BF16, kind="ExternalOutput").ap()
        if "lof" in self.taps:
            self.tapd["tlof"] = nc.dram_tensor("tlof", [128, NCH * 512], BF16, kind="ExternalOutput").ap()
        if "ofm" in self.taps:
            self.tapd["tofm"] = nc.dram_tensor("tofm", [128, 8 * 512], BF16, kind="ExternalOutput").ap()

        self.stream = {e: [] for e in "tvsqg"}
        self.hwm = {e: {} for e in "tvsqg"}
        self._wtok = {}
        self._rtok = {}
        self.out_toks = []

        self._ctx = []

        def E(cm):
            self._ctx.append(cm)
            return cm.__enter__()

        self._cleanup = nc.cleanup_on_exit()
        self._cleanup.__enter__()

        self.sem = {}
        for e in ["t", "v", "s", "q", "g", "w0", "w1", "w2", "w3",
                  "d0", "d1", "d2", "d3", "wb0", "wb1", "sg0", "sg1"]:
            self.sem[e] = E(nc.semaphore("sem_" + e))
        self.cnt = {e: 0 for e in self.sem}

        total = 212832 - 64
        self.arena_h = E(nc.sbuf_tensor("arena", [128, total], mybir.dt.uint8))
        self.base = nc.lookup_mloc(self.arena_h).addr
        self._off = 0
        self._sizes = {}

        def sb(name, cols, dtype, off=None):
            bsz = cols * (2 if dtype == BF16 else 4)
            if off is None:
                off = self._off
                self._off += (bsz + 63) // 64 * 64
            assert off + bsz <= total, (name, off, bsz, total)
            self._sizes[name] = (off, bsz)
            return nc.alloc_sbuf_tensor_at(name, [128, cols], dtype, offset=self.base + off)

        self.wo = sb("wo", 8192, BF16)
        self.ident = sb("ident", 128, BF16)
        self.cmask = sb("cmask", 2048, BF16)
        self.ones = sb("ones", 128, BF16)
        self.consts = sb("consts", 8, F32)
        self.rT = sb("rT", 64, BF16)
        self.rTf = sb("rTf", 64, F32)
        self.rTlo = sb("rTlo", 64, BF16)
        self.masktm = sb("masktm", NTILE * 8, BF16)
        self.stats = sb("stats", 32, F32)
        self.otmpf = sb("otmpf", 512, F32)
        self.recipb = sb("recipb", 512, BF16, off=self._sizes["otmpf"][0])
        self.mrow = sb("mrow", 512, BF16)
        self.otmp2 = sb("otmp2", 512, BF16, off=self._sizes["mrow"][0])
        self.tmerge = sb("tmerge", 1024, F32)
        self.sgl = sb("sgl", 1024, BF16)

        r1 = self._off
        self.wq = sb("wq", 8192, BF16)
        self.wbuf0 = sb("wbuf0", 2048, BF16, off=r1)
        self.wbuf2 = sb("wbuf2", 2048, BF16, off=r1 + 4096)
        self.ofm = sb("ofm", 4096, BF16, off=r1 + 8192)
        self.x1lofm = sb("x1lofm", NCH * 512, BF16, off=r1 + 8192)
        r2 = self._off
        self.xfm = sb("xfm", NCH * SEQ, BF16)
        r3 = self._off
        self.qfm = sb("qfm", NCH * SEQ, BF16)
        self.zfm = sb("zfm", ICN * 512, BF16, off=r3)
        self.xefm = sb("xefm", NCH * 512, BF16, off=r3 + 22528)
        r4 = self._off
        self.kfm = sb("kfm", NCH * SEQ, BF16)
        self.yacc = sb("yacc", 4 * DM, F32, off=r4)
        self.vtm = sb("vtm", NTILE * NH * 66, BF16)
        self.xres = sb("xres", 4 * DM, F32)
        r7 = self._off
        self.pts = sb("pts", NTILE * 512, BF16)
        self.wbuf1 = sb("wbuf1", 2048, BF16, off=r7)
        self.wbuf3 = sb("wbuf3", 2048, BF16, off=r7 + 4096)
        self.dnbuf = [sb(f"dnbuf{p}", 1024, BF16, off=r7 + 8192 + p * 2048) for p in range(4)]
        self.xcast = sb("xcast", 4096, BF16, off=r7)
        r6 = self._sizes["xres"][0]
        self.stg_gu = [sb(f"stg_gu{p}", 4096, BF16, off=r6 + p * 8192) for p in range(2)]
        self.stg_dn = [sb(f"stg_dn{p}", 2048, BF16, off=r7 + 8192 + p * 4096) for p in range(2)]
        assert self._off <= total, self._off

        self.pf = E(nc.psum_tensor("pf", [128, 6 * 512], F32))
        self.pb = E(nc.psum_tensor("pb", [128, 2 * 1024], BF16))
        self.wbufs = [self.wbuf0, self.wbuf1, self.wbuf2, self.wbuf3]

        self.block_cm = nc.Block()
        self.block = self.block_cm.__enter__()

        self._init_phase()
        self._a1(0)
        self._precast()
        pc_tok = dict(self.cnt)
        for eng in "tvsq":
            for sname in ("wb0", "wb1", "sg0", "sg1"):
                self._wait(eng, (sname, pc_tok[sname]))
        for b in range(nseq):
            if b > 0:
                self._a1(b)
            self._attn(b)
            self._ffn(b)
            self.barrier()
        self._finish()

        blk = self.block
        st = self.stream
        blk.gpsimd(lambda e: [f(e) for f in st["g"]])
        blk.vector(lambda e: [f(e) for f in st["v"]])
        blk.scalar(lambda e: [f(e) for f in st["s"]])
        blk.tensor(lambda e: [f(e) for f in st["t"]])
        blk.sync(lambda e: [f(e) for f in st["q"]])

        self.block_cm.__exit__(None, None, None)
        for cm in reversed(self._ctx):
            cm.__exit__(None, None, None)
        self._cleanup.__exit__(None, None, None)
        lower_extended_insts(nc)

    def _wait(self, eng, tok):
        if tok is None:
            return
        s, v = tok
        if v <= 0 or self.hwm[eng].get(s, 0) >= v:
            return
        self.hwm[eng][s] = v
        sem = self.sem[s]
        self.stream[eng].append(lambda E, sem=sem, v=v: E.wait_ge(sem, v))

    ENG_SEM = {"t": "t", "v": "v", "s": "s", "q": "q", "g": "g"}

    def op(self, eng, fn, rd=(), wr=(), extra=(), inc=True, dsem=None):
        own = self.ENG_SEM[eng]
        deps = []
        for n in rd:
            t = self._wtok.get(n)
            if t is not None:
                deps.append(t)
        for n in wr:
            t = self._wtok.get(n)
            if t is not None and t[0] != own:
                deps.append(t)
            for oe, v in self._rtok.get(n, {}).items():
                deps.append((oe, v))
        deps.extend(x for x in extra if x is not None)
        for t in deps:
            if t[0] == own and eng == "t":
                continue
            self._wait(eng, t)
        if eng in "qg":
            s = dsem or own
            self.cnt[s] += 16
            val = self.cnt[s]
            sem = self.sem[s]
            self.stream[eng].append(
                lambda E, fn=fn, sem=sem: fn(E).then_inc(sem, 16))
            if dsem is None:
                self._wait(eng, (s, val))
            tok = (s, val)
        elif inc:
            self.cnt[own] += 1
            val = self.cnt[own]
            sem = self.sem[own]
            self.stream[eng].append(
                lambda E, fn=fn, sem=sem: fn(E).then_inc(sem, 1))
            tok = (own, val)
        else:
            self.stream[eng].append(lambda E, fn=fn: fn(E))
            tok = (own, self.cnt[own])
        for n in wr:
            self._wtok[n] = tok
            self._rtok[n] = {}
        for n in rd:
            self._rtok.setdefault(n, {})[tok[0]] = tok[1]
        return tok

    def barrier(self):
        snap = dict(self.cnt)
        for eng in "tvsqg":
            for s, v in snap.items():
                if s != self.ENG_SEM[eng]:
                    self._wait(eng, (s, v))

    def lp(self, fn):
        def g(E, fn=fn):
            with self.nc.allow_low_precision(reason="bf16 operand for following matmul"):
                return fn(E)
        return g

    def bank(self, i):
        return self.pf[:, i * 512:(i + 1) * 512]

    def tbank(self, i):
        return self.pb[:, i * 1024:(i + 1) * 1024]

    def _init_phase(self):
        v = "v"
        self.op(v, lambda E: E.memset(self.consts[:, 0:1], -30.0), wr=("consts",))
        self.op(v, lambda E: E.memset(self.consts[:, 1:2], 1e-5), wr=("consts",))
        self.op(v, lambda E: E.memset(self.ones[:], 1.0), wr=("ones",))
        vr = self.vtm[:].rearrange("p (t h c) -> p t h c", h=NH, c=66)
        self.op(v, lambda E: E.memset(vr[:, :, :, 64:66], 1.0), wr=("vtm",))
        self.op("g", lambda E: E.dma_start(self.ident[:], self.identd), wr=("ident",))
        self.op("g", lambda E: E.dma_start(self.cmask[:], self.cmaskd), wr=("cmask",))
        self.op("g", lambda E: E.dma_start(
            self.rT[:].rearrange("p (c e) -> p c e", e=8),
            self.rTd.rearrange("(c p) e -> p c e", p=128)), wr=("rT",))
        self.op("g", lambda E: E.dma_start(
            self.rTf[:].rearrange("p (c e) -> p c e", e=8),
            self.rTd.rearrange("(c p) e -> p c e", p=128)), wr=("rTf",))
        self.op("v", self.lp(lambda E: E.tensor_tensor(
            self.rTlo[:], self.rTf[:], self.rT[:], ALU.subtract)),
            rd=("rTf", "rT"), wr=("rTlo",))
        self.op("g", lambda E: E.dma_start(
            self.wo[:].rearrange("p (j n) -> p j n", j=8), self.woTp), wr=("wo",))

    def _precast(self):
        for e in range(NE):
            gT_e = self.gTd[e].rearrange("(c p) i -> p c i", p=128)
            uT_e = self.uTd[e].rearrange("(c p) i -> p c i", p=128)
            for grp in range(WG):
                par = grp % 2
                sg, sd = self.stg_gu[par], self.stg_dn[par]
                ng, nd = f"stg_gu{par}", f"stg_dn{par}"
                for sub in range(2):
                    ic = grp * 2 + sub
                    i0 = ic * 128
                    self.op("g", lambda E, sg=sg, s=sub, gT_e=gT_e, i0=i0: E.dma_start(
                        sg[:, s * 2048:s * 2048 + 1024].rearrange("p (c i) -> p c i", c=8),
                        gT_e[:, :, i0:i0 + 128]), wr=(ng,), dsem=f"sg{par}")
                    self.op("g", lambda E, sg=sg, s=sub, uT_e=uT_e, i0=i0: E.dma_start(
                        sg[:, s * 2048 + 1024:s * 2048 + 2048].rearrange("p (c i) -> p c i", c=8),
                        uT_e[:, :, i0:i0 + 128]), wr=(ng,), dsem=f"sg{par}")
                    self.op("g", lambda E, sd=sd, s=sub, e=e, i0=i0: E.dma_start(
                        sd[:, s * 1024:s * 1024 + 1024],
                        self.dTd[e, i0:i0 + 128, :]), wr=(nd,), dsem=f"sg{par}")
                self.op("q", lambda E, e=e, grp=grp, sg=sg: E.dma_start(
                    self.wgu[e, grp], sg[:]), rd=(ng,), wr=(f"wgu{e}_{grp}",),
                    dsem=f"wb{par}")
                self.op("q", lambda E, e=e, grp=grp, sd=sd: E.dma_start(
                    self.wdn[e, grp], sd[:]), rd=(nd,), wr=(f"wdn{e}_{grp}",),
                    dsem=f"wb{par}")

    def _a1(self, b):
        nc = self.nc
        XCN = tuple(f"pts{t}" for t in range(8)) + ("xcast",)
        for g in range(NSTRIP):
            src = self.xin[b, g * 512:(g + 1) * 512, :].rearrange("(t p) d -> p t d", p=128)
            self.op("g", lambda E, src=src: E.dma_start(
                self.xcast[:].rearrange("p (t d) -> p t d", t=4), src), wr=XCN)
            for c in range(NCH):
                tb = (g * NCH + c) % 2
                tname = f"T{tb}"
                for t in range(4):
                    self.op("t", lambda E, tb=tb, t=t, c=c: E.transpose(
                        self.tbank(tb)[:, t * 128:(t + 1) * 128],
                        self.xcast[:, t * 1024 + c * 128:t * 1024 + c * 128 + 128],
                        self.ident[:]),
                        rd=XCN + ("ident",), wr=(tname,), inc=(t == 3))
                self.op("v", lambda E, tb=tb, c=c, g=g: E.tensor_copy(
                    self.xfm[:, c * SEQ + g * 512:c * SEQ + g * 512 + 512],
                    self.tbank(tb)[:, 0:512]), rd=(tname,), wr=("xfm",))
        for wname, dst, dstn, wsrc in (("wq_q", self.qfm, "qfm", self.wqT),
                                       ("wq_k", self.kfm, "kfm", self.wkT)):
            self.op("g", lambda E, wsrc=wsrc: E.dma_start(
                self.wq[:].rearrange("p (c n) -> p c n", c=8),
                wsrc.rearrange("(c p) n -> p c n", p=128)), wr=("wq",))
            for c in range(NCH):
                for d in range(NCH):
                    for g in range(NSTRIP):
                        self.op("t", lambda E, g=g, d=d, c=c: E.matmul(
                            self.bank(g),
                            self.wq[:, d * 1024 + c * 128:d * 1024 + c * 128 + 128],
                            self.xfm[:, d * SEQ + g * 512:d * SEQ + g * 512 + 512],
                            start=(d == 0), stop=(d == NCH - 1), skip_group_check=True),
                            rd=("wq", "xfm"), wr=(f"B{g}",),
                            inc=(d == NCH - 1))
                for g in range(NSTRIP):
                    self.op("v", lambda E, g=g, c=c, dst=dst: E.tensor_copy(
                        dst[:, c * SEQ + g * 512:c * SEQ + g * 512 + 512],
                        self.bank(g)), rd=(f"B{g}",), wr=(dstn,))
        self.op("g", lambda E: E.dma_start(
            self.wq[:].rearrange("p (c n) -> p c n", c=8),
            self.wvT.rearrange("(c p) n -> p c n", p=128)), wr=("wq",))
        vr = self.vtm[:].rearrange("p (t h c) -> p t h c", h=NH, c=66)
        for t in range(NTILE):
            pair = (4, 5) if t % 2 == 0 else (2, 3)
            for d in range(NCH):
                for hlf in range(2):
                    self.op("t", lambda E, pair=pair, hlf=hlf, d=d, t=t: E.matmul(
                        self.bank(pair[hlf]),
                        self.xfm[:, d * SEQ + t * 128:d * SEQ + t * 128 + 128],
                        self.wq[:, d * 1024 + hlf * 512:d * 1024 + hlf * 512 + 512],
                        start=(d == 0), stop=(d == NCH - 1), skip_group_check=True),
                        rd=("wq", "xfm"), wr=(f"B{pair[hlf]}",),
                        inc=(d == NCH - 1))
            for hlf in range(2):
                self.op("v", lambda E, pair=pair, hlf=hlf, t=t, vr=vr: E.tensor_copy(
                    vr[:, t, hlf * 8:hlf * 8 + 8, 0:64],
                    self.bank(pair[hlf]).rearrange("p (h c) -> p h c", c=64)),
                    rd=(f"B{pair[hlf]}",), wr=("vtm",))
        if "qkv" in self.taps and b == 0:
            self.op("q", lambda E: E.dma_start(self.tapd["tq"], self.qfm[:]), rd=("qfm",))
            self.op("q", lambda E: E.dma_start(self.tapd["tk"], self.kfm[:]), rd=("kfm",))
            tok = self.op("q", lambda E: E.dma_start(self.tapd["tv"], self.vtm[:]), rd=("vtm",))
            self.out_toks.append(tok)

    def _attn(self, b):
        cneg = self.consts[:, 0:1]
        ceps = self.consts[:, 1:2]
        for g in range(NSTRIP):
            src = self.xin[b, g * 512:(g + 1) * 512, :].rearrange("(t p) d -> p t d", p=128)
            self.op("q", lambda E, src=src: E.dma_start(
                self.xres[:].rearrange("p (t d) -> p t d", t=4), src), wr=("xres",))
            T = 4 * g + 4

            def opr_tile(h, t):
                self.op("t", lambda E, t=t, h=h: E.matmul(
                    self.bank(2)[0:66, :],
                    self.vtm[:, t * 1056 + h * 66:t * 1056 + h * 66 + 66],
                    self.pts[:, t * 512:(t + 1) * 512],
                    start=(t == 0), stop=(t == T - 1), skip_group_check=True),
                    rd=("vtm", f"pts{t}"), wr=("B2",))

            def head_scores(h):
                hr = (h % 2) * 64
                ch = h // 2
                for t0 in range(0, T, 2):
                    qd = (t0 // 2) % 2
                    bks = (0, 1) if qd == 0 else (4, 5)
                    for j, t in enumerate((t0, t0 + 1)):
                        self.op("t", lambda E, bk=bks[j], hr=hr, ch=ch, t=t, g=g: E.matmul(
                            self.bank(bk),
                            self.kfm[hr:hr + 64, ch * SEQ + t * 128:ch * SEQ + t * 128 + 128],
                            self.qfm[hr:hr + 64, ch * SEQ + g * 512:ch * SEQ + g * 512 + 512],
                            start=True, stop=True, tile_position=(hr, 0),
                            skip_group_check=True),
                            rd=("kfm", "qfm"), wr=(f"B{bks[j]}",))
                    off = 0 if qd == 0 else 2048
                    self.op("s", lambda E, t0=t0, off=off: E.activation(
                        self.pts[:, t0 * 512:(t0 + 2) * 512],
                        self.pf[:, off:off + 1024],
                        AF.Exp, bias=cneg, scale=0.125),
                        rd=(f"B{bks[0]}", f"B{bks[1]}", "consts"),
                        wr=(f"pts{t0}", f"pts{t0 + 1}"))
                    for t in (t0, t0 + 1):
                        r = t - 4 * g
                        if r >= 0:
                            self.op("v", lambda E, t=t, r=r: E.tensor_tensor(
                                self.pts[:, t * 512:(t + 1) * 512],
                                self.pts[:, t * 512:(t + 1) * 512],
                                self.cmask[:, r * 512:(r + 1) * 512], ALU.mult),
                                rd=(f"pts{t}", "cmask"), wr=(f"pts{t}",))

            def head_opr(h):
                for t in range(T):
                    opr_tile(h, t)
                self.op("v", lambda E: E.tensor_copy(
                    self.otmpf[0:64, :], self.bank(2)[0:64, :]),
                    rd=("B2",), wr=("otmpf",))
                self.op("v", self.lp(lambda E: E.reciprocal(
                    self.recipb[64:66, 0:512], self.bank(2)[64:66, :])),
                    rd=("B2",), wr=("recipb",))

            def head_bcast(h):
                self.op("t", lambda E: E.matmul(
                    self.bank(3), self.ones[64:65, 0:128], self.recipb[64:65, 0:512],
                    start=True, stop=True, tile_position=(64, 0), skip_group_check=True),
                    rd=("ones", "recipb"), wr=("B3",))

            def head_mul(h):
                if h < 8:
                    self.op("v", lambda E, h=h: E.tensor_tensor(
                        self.ofm[0:64, h * 512:(h + 1) * 512],
                        self.otmpf[0:64, :], self.bank(3)[0:64, :], ALU.mult),
                        rd=("otmpf", "B3"), wr=("ofm",))
                else:
                    self.op("v", lambda E: E.tensor_tensor(
                        self.otmp2[0:64, :], self.otmpf[0:64, :],
                        self.bank(3)[0:64, :], ALU.mult),
                        rd=("otmpf", "B3"), wr=("otmp2",))

            def head_shift(h):
                if h >= 8:
                    self.op("t", lambda E: E.matmul(
                        self.bank(3)[64:128, :], self.ident[0:64, 0:64],
                        self.otmp2[0:64, 0:512], start=True, stop=True,
                        tile_position=(0, 64), skip_group_check=True),
                        rd=("ident", "otmp2"), wr=("B3",))
                    self.op("v", lambda E, h=h: E.tensor_copy(
                        self.ofm[64:128, (h - 8) * 512:(h - 7) * 512],
                        self.bank(3)[64:128, :]), rd=("B3",), wr=("ofm",))

            for h in range(NH):
                head_scores(h)
                if h >= 1:
                    head_bcast(h - 1)
                    head_mul(h - 1)
                head_opr(h)
                if h >= 1:
                    head_shift(h - 1)
            head_bcast(NH - 1)
            head_mul(NH - 1)
            head_shift(NH - 1)
            if "ofm" in self.taps and b == 0 and g == 0:
                tok = self.op("q", lambda E: E.dma_start(self.tapd["tofm"], self.ofm[:]), rd=("ofm",))
                self.out_toks.append(tok)
            for qt in range(4):
                for j in range(NCH):
                    for hlf in range(2):
                        bk0, bk1 = (2, 3) if hlf == 0 else (4, 5)
                        self.op("t", lambda E, j=j, hlf=hlf, qt=qt, bk0=bk0: E.matmul(
                            self.bank(bk0),
                            self.ofm[0:64, j * 512 + qt * 128:j * 512 + qt * 128 + 128],
                            self.wo[0:64, j * 1024 + hlf * 512:j * 1024 + hlf * 512 + 512],
                            start=(j == 0), stop=(j == NCH - 1),
                            tile_position=(0, 0), skip_group_check=True),
                            rd=("ofm", "wo"), wr=(f"B{bk0}",), inc=(j == NCH - 1))
                        self.op("t", lambda E, j=j, hlf=hlf, qt=qt, bk1=bk1: E.matmul(
                            self.bank(bk1),
                            self.ofm[64:128, j * 512 + qt * 128:j * 512 + qt * 128 + 128],
                            self.wo[64:128, j * 1024 + hlf * 512:j * 1024 + hlf * 512 + 512],
                            start=(j == 0), stop=(j == NCH - 1),
                            tile_position=(64, 0), skip_group_check=True),
                            rd=("ofm", "wo"), wr=(f"B{bk1}",), inc=(j == NCH - 1))
                xt = self.xres[:, qt * 1024:(qt + 1) * 1024]
                self.op("v", lambda E, xt=xt: E.tensor_tensor(
                    self.tmerge[:, 0:512], xt[:, 0:512], self.bank(2), ALU.add),
                    rd=("B2", "xres"), wr=("tmerge",))
                self.op("v", lambda E: E.tensor_tensor(
                    self.tmerge[:, 0:512], self.tmerge[:, 0:512], self.bank(3), ALU.add),
                    rd=("B3", "tmerge"), wr=("tmerge",))
                self.op("v", lambda E, xt=xt: E.tensor_tensor(
                    self.tmerge[:, 512:1024], xt[:, 512:1024], self.bank(4), ALU.add),
                    rd=("B4", "xres"), wr=("tmerge",))
                self.op("v", lambda E: E.tensor_tensor(
                    self.tmerge[:, 512:1024], self.tmerge[:, 512:1024], self.bank(5), ALU.add),
                    rd=("B5", "tmerge"), wr=("tmerge",))
                self.op("v", lambda E, xt=xt, qt=qt: E.tensor_tensor_reduce(
                    out=xt, in0=self.tmerge[:], in1=self.tmerge[:], scale=0.5,
                    scalar=0.0, op0=ALU.add, op1=ALU.add,
                    accum_out=self.stats[:, qt:qt + 1]),
                    rd=("tmerge",), wr=("xres", "stats"))
                self.op("s", lambda E, xt=xt, qt=qt: E.activation(
                    self.tmerge[:], xt, AF.Square,
                    accum_out=self.stats[:, 4 + qt:5 + qt]),
                    rd=("xres",), wr=("tmerge", "stats"))
            self._ln(self.stats, ceps)
            xfm_r = self.xfm[:].rearrange("p (c s) -> p c s", s=SEQ)
            for qt in range(4):
                xt = self.xres[:, qt * 1024:(qt + 1) * 1024]
                tn = (f"pts{2 * qt}", f"pts{2 * qt + 1}", "xcast")
                self.op("s", lambda E, xt=xt, qt=qt: E.activation(
                    xt, xt, AF.Identity, bias=self.stats[:, 12 + qt:13 + qt],
                    scale=self.stats[:, 24 + qt:25 + qt]),
                    rd=("stats", "xres"), wr=("xres",))
                self.op("v", lambda E, xt=xt, qt=qt: E.tensor_copy(
                    self.xcast[:, qt * 1024:(qt + 1) * 1024], xt),
                    rd=("xres",), wr=tn)
                for c in range(NCH):
                    tb = c // 4
                    self.op("t", lambda E, tb=tb, qt=qt, c=c: E.transpose(
                        self.tbank(tb)[:, (c % 4) * 128:(c % 4) * 128 + 128],
                        self.xcast[:, qt * 1024 + c * 128:qt * 1024 + c * 128 + 128],
                        self.ident[:]),
                        rd=tn + ("ident",), wr=(f"T{tb}",), inc=(c % 4 == 3))
                for tb in range(2):
                    self.op("v", lambda E, tb=tb, qt=qt, g=g, xfm_r=xfm_r: E.tensor_copy(
                        xfm_r[:, tb * 4:tb * 4 + 4, g * 512 + qt * 128:g * 512 + qt * 128 + 128],
                        self.tbank(tb)[:, 0:512].rearrange("p (c k) -> p c k", k=128)),
                        rd=(f"T{tb}",), wr=("xfm",))
            dst = self.x1scr[g * 512:(g + 1) * 512, :].rearrange("(t p) d -> p t d", p=128)
            tok = self.op("q", lambda E, dst=dst: E.dma_start(
                dst, self.xres[:].rearrange("p (t d) -> p t d", t=4)),
                rd=("xres",), wr=(f"x1s{g}",))
            if "x1" in self.taps and b == 0:
                dst2 = self.tapd["tx1"][g * 512:(g + 1) * 512, :].rearrange("(t p) d -> p t d", p=128)
                tok = self.op("q", lambda E, dst2=dst2: E.dma_start(
                    dst2, self.xres[:].rearrange("p (t d) -> p t d", t=4)), rd=("xres",))
                self.out_toks.append(tok)
            XCN = tuple(f"pts{t}" for t in range(8)) + ("xcast",)
            self.op("v", self.lp(lambda E: E.tensor_tensor(
                self.xcast[:], self.xres[:], self.xcast[:], ALU.subtract)),
                rd=("xres",) + XCN, wr=XCN)
            for c in range(NCH):
                tb = c % 2
                for t in range(4):
                    self.op("t", lambda E, tb=tb, t=t, c=c: E.transpose(
                        self.tbank(tb)[:, t * 128:(t + 1) * 128],
                        self.xcast[:, t * 1024 + c * 128:t * 1024 + c * 128 + 128],
                        self.ident[:]),
                        rd=XCN + ("ident",), wr=(f"T{tb}",), inc=(t == 3))
                self.op("v", lambda E, tb=tb, c=c: E.tensor_copy(
                    self.x1lofm[:, c * 512:(c + 1) * 512],
                    self.tbank(tb)[:, 0:512]), rd=(f"T{tb}",), wr=("x1lofm", "ofm"))
            if "lof" in self.taps and b == 0 and g == 0:
                tok = self.op("q", lambda E: E.dma_start(self.tapd["tlof"], self.x1lofm[:]),
                              rd=("x1lofm", "ofm"))
                self.out_toks.append(tok)
            for qt in range(4):
                Q = g * 4 + qt
                nmm = 3 * NCH
                i = 0
                for lhs_sel, rhs_sel in (("hi", "rT"), ("hi", "rTlo"), ("lo", "rT")):
                    for c in range(NCH):
                        if lhs_sel == "hi":
                            lap = self.xfm[:, c * SEQ + Q * 128:c * SEQ + Q * 128 + 128]
                            lrd = "xfm"
                        else:
                            lap = self.x1lofm[:, c * 512 + qt * 128:c * 512 + qt * 128 + 128]
                            lrd = "ofm"
                        rap = (self.rT if rhs_sel == "rT" else self.rTlo)[:, c * 8:c * 8 + 8]
                        i += 1
                        self.op("t", lambda E, qt=qt, lap=lap, rap=rap, i=i, nmm=nmm: E.matmul(
                            self.bank(0)[:, qt * 8:qt * 8 + 8], lap, rap,
                            start=(i == 1), stop=(i == nmm), skip_group_check=True),
                            rd=(lrd, rhs_sel), wr=("B0",), inc=(i == nmm))
                self.op("v", lambda E, qt=qt: E.reduce_max(
                    self.stats[:, 28 + qt:29 + qt], self.bank(0)[:, qt * 8:qt * 8 + 8],
                    axis=AX.X), rd=("B0",), wr=("stats",))
                self.op("v", lambda E, qt=qt, Q=Q: E.tensor_scalar(
                    self.masktm[:, Q * 8:Q * 8 + 8], self.bank(0)[:, qt * 8:qt * 8 + 8],
                    self.stats[:, 28 + qt:29 + qt], None, ALU.is_ge),
                    rd=("B0", "stats"), wr=("masktm",))
        if "mask" in self.taps and b == 0:
            tok = self.op("q", lambda E: E.dma_start(self.tapd["tmask"], self.masktm[:]),
                          rd=("masktm",))
            self.out_toks.append(tok)

    def _ln(self, stats, ceps):
        self.op("v", lambda E: E.tensor_scalar_mul(
            stats[:, 8:12], stats[:, 0:4], -1.0 / DM), rd=("stats",), wr=("stats",))
        self.op("v", lambda E: E.tensor_tensor(
            stats[:, 12:16], stats[:, 8:12], stats[:, 8:12], ALU.mult),
            rd=("stats",), wr=("stats",))
        self.op("v", lambda E: E.scalar_tensor_tensor(
            stats[:, 16:20], stats[:, 4:8], 1.0 / DM, stats[:, 12:16],
            ALU.mult, ALU.subtract), rd=("stats",), wr=("stats",))
        self.op("s", lambda E: E.activation(
            stats[:, 20:24], stats[:, 16:20], AF.Sqrt, bias=ceps),
            rd=("stats", "consts"), wr=("stats",))
        self.op("v", lambda E: E.reciprocal(
            stats[:, 24:28], stats[:, 20:24]), rd=("stats",), wr=("stats",))
        self.op("v", lambda E: E.tensor_tensor(
            stats[:, 12:16], stats[:, 8:12], stats[:, 24:28], ALU.mult),
            rd=("stats",), wr=("stats",))

    def _ffn(self, b):
        ceps = self.consts[:, 1:2]
        self._wait("q", ("t", self.cnt["t"]))
        self._wait("q", ("v", self.cnt["v"]))
        def reload(g):
            src = self.x1scr[g * 512:(g + 1) * 512, :].rearrange("(t p) d -> p t d", p=128)
            self.op("q", lambda E, src=src: E.dma_start(
                self.xres[:].rearrange("p (t d) -> p t d", t=4), src),
                rd=(f"x1s{g}",), wr=("xres",))

        reload(0)
        for g in range(NSTRIP):
            def prep(e, g=g):
                for qt in range(4):
                    Q = g * 4 + qt
                    self.op("t", lambda E, qt=qt, Q=Q, e=e: E.transpose(
                        self.tbank(0)[0:1, qt * 128:qt * 128 + 128],
                        self.masktm[:, Q * 8 + e:Q * 8 + e + 1], self.ident[:]),
                        rd=("masktm", "ident"), wr=("T0",), inc=(qt == 3))
                self.op("v", lambda E: E.tensor_copy(
                    self.mrow[0:1, 0:512], self.tbank(0)[0:1, 0:512]),
                    rd=("T0",), wr=("mrow",))
                self.op("t", lambda E: E.matmul(
                    self.bank(0), self.ones[0:1, 0:128], self.mrow[0:1, 0:512],
                    start=True, stop=True, skip_group_check=True),
                    rd=("ones", "mrow"), wr=("B0",))
                for c in range(NCH):
                    self.op("v", lambda E, c=c, g=g: E.tensor_tensor(
                        self.xefm[:, c * 512:(c + 1) * 512],
                        self.xfm[:, c * SEQ + g * 512:c * SEQ + g * 512 + 512],
                        self.bank(0), ALU.mult), rd=("xfm", "B0"), wr=("xefm",))

            if g == 0:
                prep(0)
            for e in range(NE):
                for ic in range(ICN):
                    par = ic % 4
                    wb = self.wbufs[par]
                    wn = f"wbuf{par}"
                    wsl = self.wgu[e, ic // 2][:, (ic % 2) * 2048:(ic % 2) * 2048 + 2048]
                    self.op("q" if par % 2 == 0 else "g",
                        lambda E, wb=wb, wsl=wsl: E.dma_start(wb[:], wsl),
                        rd=(f"wgu{e}_{ic // 2}",), wr=(wn,), dsem=f"w{par}")
                    gbk, ubk = (0, 1) if ic % 2 == 0 else (2, 3)
                    for d in range(NCH):
                        self.op("t", lambda E, wb=wb, d=d, gbk=gbk: E.matmul(
                            self.bank(gbk),
                            wb[:, d * 128:d * 128 + 128],
                            self.xefm[:, d * 512:(d + 1) * 512],
                            start=(d == 0), stop=(d == NCH - 1), skip_group_check=True),
                            rd=(wn, "xefm"), wr=(f"B{gbk}",), inc=(d == NCH - 1))
                    for d in range(NCH):
                        self.op("t", lambda E, wb=wb, d=d, ubk=ubk: E.matmul(
                            self.bank(ubk),
                            wb[:, 1024 + d * 128:1024 + d * 128 + 128],
                            self.xefm[:, d * 512:(d + 1) * 512],
                            start=(d == 0), stop=(d == NCH - 1), skip_group_check=True),
                            rd=(wn, "xefm"), wr=(f"B{ubk}",), inc=(d == NCH - 1))
                    sp = ic % 2
                    self.op("s", lambda E, gbk=gbk, sp=sp: E.activation(
                        self.sgl[:, sp * 512:(sp + 1) * 512], self.bank(gbk), AF.Silu),
                        rd=(f"B{gbk}",), wr=(f"sgl{sp}",))
                    self.op("v", lambda E, ic=ic, sp=sp, ubk=ubk: E.tensor_tensor(
                        self.zfm[:, ic * 512:(ic + 1) * 512],
                        self.sgl[:, sp * 512:(sp + 1) * 512],
                        self.bank(ubk), ALU.mult),
                        rd=(f"sgl{sp}", f"B{ubk}"), wr=("zfm",))
                if e + 1 < NE:
                    prep(e + 1)
                elif g + 1 < NSTRIP:
                    prep(0, g + 1)
                for hlf in range(2):
                    for grp in range(WG):
                        par = (hlf * WG + grp) % 4
                        db = self.dnbuf[par]
                        dn = f"dnbuf{par}"
                        wsl = self.wdn[e, grp].rearrange(
                            "p (s c) -> p s c", s=2)[:, :, hlf * 512:hlf * 512 + 512]
                        self.op("q" if par == 0 else "g",
                            lambda E, db=db, wsl=wsl: E.dma_start(
                                db[:, 0:1024].rearrange("p (s c) -> p s c", s=2), wsl),
                            rd=(f"wdn{e}_{grp}",), wr=(dn,), dsem=f"d{par}")
                        for sub in range(2):
                            ic = grp * 2 + sub
                            for tt in range(4):
                                bk = 2 + tt
                                self.op("t", lambda E, ic=ic, tt=tt, db=db,
                                        sub=sub, bk=bk: E.matmul(
                                    self.bank(bk),
                                    self.zfm[:, ic * 512 + tt * 128:ic * 512 + tt * 128 + 128],
                                    db[:, sub * 512:sub * 512 + 512],
                                    start=(ic == 0), stop=(ic == ICN - 1),
                                    skip_group_check=True),
                                    rd=(dn, "zfm"), wr=(f"B{bk}",))
                    for tt in range(4):
                        bk = 2 + tt
                        dstc = tt * 1024 + hlf * 512
                        if e == 0:
                            self.op("v", lambda E, bk=bk, dstc=dstc: E.tensor_copy(
                                self.yacc[:, dstc:dstc + 512], self.bank(bk)),
                                rd=(f"B{bk}",), wr=("yacc",))
                        else:
                            self.op("v", lambda E, bk=bk, dstc=dstc: E.tensor_tensor(
                                self.yacc[:, dstc:dstc + 512],
                                self.yacc[:, dstc:dstc + 512],
                                self.bank(bk), ALU.add),
                                rd=(f"B{bk}", "yacc"), wr=("yacc",))
            for qt in range(4):
                xt = self.xres[:, qt * 1024:(qt + 1) * 1024]
                self.op("v", lambda E, xt=xt, qt=qt: E.tensor_tensor_reduce(
                    out=xt, in0=xt, in1=self.yacc[:, qt * 1024:(qt + 1) * 1024],
                    scale=1.0, scalar=0.0, op0=ALU.add, op1=ALU.add,
                    accum_out=self.stats[:, qt:qt + 1]),
                    rd=("yacc", "xres"), wr=("xres", "stats"))
                self.op("s", lambda E, xt=xt, qt=qt: E.activation(
                    self.tmerge[:], xt, AF.Square,
                    accum_out=self.stats[:, 4 + qt:5 + qt]),
                    rd=("xres",), wr=("tmerge", "stats"))
            self._ln(self.stats, ceps)
            for qt in range(4):
                xt = self.xres[:, qt * 1024:(qt + 1) * 1024]
                self.op("s", lambda E, xt=xt, qt=qt: E.activation(
                    xt, xt, AF.Identity, bias=self.stats[:, 12 + qt:13 + qt],
                    scale=self.stats[:, 24 + qt:25 + qt]),
                    rd=("stats", "xres"), wr=("xres",))
            dst = self.outd[b, g * 512:(g + 1) * 512, :].rearrange("(t p) d -> p t d", p=128)
            tok = self.op("q", lambda E, dst=dst: E.dma_start(
                dst, self.xres[:].rearrange("p (t d) -> p t d", t=4)), rd=("xres",))
            self.out_toks.append(tok)
            if g + 1 < NSTRIP:
                reload(g + 1)

    def _finish(self):
        for tok in self.out_toks:
            self._wait("q", tok)
        self.barrier()


def _host_prep(L0_attn_X, L0_attn_q_proj, L0_attn_k_proj, L0_attn_v_proj,
               L0_attn_o_proj, L0_ffn_router_w, L0_ffn_gate_w, L0_ffn_up_w,
               L0_ffn_down_w):
    X = np.ascontiguousarray(np.asarray(L0_attn_X, dtype=np.float32))
    wqT = np.ascontiguousarray(np.asarray(L0_attn_q_proj, dtype=np.float32).T)
    wkT = np.ascontiguousarray(np.asarray(L0_attn_k_proj, dtype=np.float32).T)
    wvT = np.ascontiguousarray(np.asarray(L0_attn_v_proj, dtype=np.float32).T)
    woT = np.asarray(L0_attn_o_proj, dtype=np.float32).T
    w16 = woT.reshape(16, 64, DM)
    woTp = np.ascontiguousarray(np.concatenate(
        [w16[0:8].transpose(1, 0, 2), w16[8:16].transpose(1, 0, 2)], axis=0))
    rT = np.ascontiguousarray(np.asarray(L0_ffn_router_w, dtype=np.float32).T)
    gT = np.ascontiguousarray(np.asarray(L0_ffn_gate_w, dtype=np.float32).transpose(0, 2, 1))
    uT = np.ascontiguousarray(np.asarray(L0_ffn_up_w, dtype=np.float32).transpose(0, 2, 1))
    dT = np.ascontiguousarray(np.asarray(L0_ffn_down_w, dtype=np.float32).transpose(0, 2, 1))
    ident = np.eye(128, dtype=BFNP)
    cm = np.zeros((128, 4 * 512), dtype=BFNP)
    qq = np.arange(512)[None, :]
    ss = np.arange(128)[:, None]
    for r in range(4):
        cm[:, r * 512:(r + 1) * 512] = (qq >= r * 128 + ss).astype(BFNP)
    shared = {"wqT": wqT, "wkT": wkT, "wvT": wvT, "woTp": woTp, "rTd": rT,
              "gTd": gT, "uTd": uT, "dTd": dT, "identd": ident, "cmaskd": cm}
    return X, shared


def kernel(**inputs):
    X, shared = _host_prep(**inputs)
    mk = MK(nseq=4)
    in_maps = []
    for c in range(NCORES):
        m = dict(shared)
        m["xin"] = np.ascontiguousarray(X[c * 4:(c + 1) * 4])
        in_maps.append(m)
    res = run_bass_kernel_spmd(mk.nc, in_maps, core_ids=list(range(NCORES)))
    return np.concatenate([res.results[c]["outd"] for c in range(NCORES)], axis=0)
